# version 31
# baseline (speedup 1.0000x reference)
import numpy as np
import ml_dtypes
import concourse.bass as bass
import concourse.mybir as mybir
from concourse.bass_utils import run_bass_kernel_spmd
from contextlib import ExitStack

F32 = mybir.dt.float32
BF16 = mybir.dt.bfloat16
I32 = mybir.dt.int32
AF = mybir.ActivationFunctionType
AL = mybir.AluOpType
AX = mybir.AxisListType

B, S, D, H, DH, FF = 4, 4096, 1024, 16, 64, 4096
OWN = S // 2
FULL = S
NT_FULL = FULL // 128
NT_OWN = OWN // 128
KC = D // 128
NQ = 4

ENG = ("sync", "vector", "scalar", "tensor", "gpsimd")


class Prog:

    def __init__(self):
        self.steps = []
        self.step_dkey = []
        self.n = {e: 0 for e in ENG}
        self.dsem = {}
        self.w = {}
        self.r = {}

    def _sched(self, eng, fn, reads, writes, kind):
        waits = {}

        def need(tok):
            if tok is None:
                return
            sem, val = tok
            if kind == "eng" and sem == eng:
                return
            if sem.startswith("D:"):
                val = self.dsem[sem]
            if waits.get(sem, -1) < val:
                waits[sem] = val

        for b in reads:
            need(self.w.get(b))
        for b in writes:
            need(self.w.get(b))
            for t in self.r.get(b, ()):
                need(t)
        if kind == "eng":
            self.n[eng] += 1
            tok = (eng, self.n[eng])
            self.step_dkey.append(None)
        else:
            key = "D:" + self._pool(writes[0])
            self.dsem[key] = self.dsem.get(key, 0) + 1
            tok = (key, self.dsem[key])
            self.step_dkey.append(key)
        for b in reads:
            self.r.setdefault(b, []).append(tok)
        for b in writes:
            self.w[b] = tok
            self.r[b] = []
        self.steps.append((eng, fn, tuple(waits.items()), kind))
        return tok

    @staticmethod
    def _pool(buf):
        if buf.startswith("d_"):
            return buf.rstrip("0123456789")
        return buf

    def run(self, eng, fn, reads=(), writes=()):
        return self._sched(eng, fn, reads, writes, "eng")

    def dma(self, q, fn, reads=(), writes=(), n=1):
        assert writes, "dma step needs a destination buffer name"
        return self._sched(q, fn, reads, writes, "dma")


def build_nc():
    nc = bass.Bass("TRN2", target_bir_lowering=False, detect_race_conditions=False)

    xr = nc.dram_tensor("xr", [FULL, D], F32, kind="ExternalInput").ap()
    dWq = nc.dram_tensor("Wq", [D, D], F32, kind="ExternalInput").ap()
    dWk = nc.dram_tensor("Wk", [D, D], F32, kind="ExternalInput").ap()
    dWv = nc.dram_tensor("Wv", [D, D], F32, kind="ExternalInput").ap()
    dWo = nc.dram_tensor("Wo", [D, D], F32, kind="ExternalInput").ap()
    dW1 = nc.dram_tensor("W1", [D, FF], F32, kind="ExternalInput").ap()
    dW2 = nc.dram_tensor("W2", [FF, D], F32, kind="ExternalInput").ap()
    dbq = nc.dram_tensor("bq", [1, D], F32, kind="ExternalInput").ap()
    dbk = nc.dram_tensor("bk", [1, D], F32, kind="ExternalInput").ap()
    dbv = nc.dram_tensor("bv", [1, D], F32, kind="ExternalInput").ap()
    dbo = nc.dram_tensor("bo", [1, D], F32, kind="ExternalInput").ap()
    db1 = nc.dram_tensor("b1", [1, FF], F32, kind="ExternalInput").ap()
    db2 = nc.dram_tensor("b2", [1, D], F32, kind="ExternalInput").ap()
    dg1 = nc.dram_tensor("g1", [1, D], F32, kind="ExternalInput").ap()
    dbe1 = nc.dram_tensor("be1", [1, D], F32, kind="ExternalInput").ap()
    dg2 = nc.dram_tensor("g2", [1, D], F32, kind="ExternalInput").ap()
    dbe2 = nc.dram_tensor("be2", [1, D], F32, kind="ExternalInput").ap()
    dPT = nc.dram_tensor("PT", [D, 256], I32, kind="ExternalInput").ap()
    dws = nc.dram_tensor("ws", [1, 1], F32, kind="ExternalInput").ap()
    dbf = nc.dram_tensor("bf", [1, D], F32, kind="ExternalInput").ap()
    did = nc.dram_tensor("id16", [128, 128], BF16, kind="ExternalInput").ap()

    dx1t_h = nc.dram_tensor("x1t_h", [OWN, D], BF16, kind="Internal").ap()
    dx1t_l = nc.dram_tensor("x1t_l", [OWN, D], BF16, kind="Internal").ap()
    dx1f_h = nc.dram_tensor("x1f_h", [NT_OWN, 128, D], BF16, kind="Internal").ap()
    dx1f_l = nc.dram_tensor("x1f_l", [NT_OWN, 128, D], BF16, kind="Internal").ap()
    dyacc = nc.dram_tensor("yacc", [OWN, D], F32, kind="Internal").ap()
    dqt = nc.dram_tensor("dqt", [OWN, D], BF16, kind="Internal").ap()
    dxmid = nc.dram_tensor("dxmid", [OWN, D], F32, kind="Internal").ap()

    dout = nc.dram_tensor("out", [OWN, D], F32, kind="ExternalOutput").ap()

    es = ExitStack()
    sb = lambda name, shape, dt: es.enter_context(nc.sbuf_tensor(name, shape, dt))
    ps = lambda name, shape, dt: es.enter_context(nc.psum_tensor(name, shape, dt))

    id16 = sb("id16s", [128, 128], BF16)
    ones1 = sb("ones1", [1, 128], F32)
    ones1b = sb("ones1b", [1, 128], BF16)
    onescol = sb("onescol", [1, 1], F32)
    rowstage = sb("rowstage", [1, 1024], F32)
    rowh = sb("rowh", [1, 1024], BF16)
    rowl = sb("rowl", [1, 1024], BF16)
    rowh2 = sb("rowh2", [1, 1024], BF16)
    rowl2 = sb("rowl2", [1, 1024], BF16)
    rep = [sb(f"rep{i}", [128, D], F32) for i in range(3)]
    b1cols = sb("b1cols", [128, FF // 128], F32)
    wscol = sb("wscol", [128, 1], F32)
    m1s = sb("m1s", [128, NT_FULL], F32)
    rs1 = sb("rs1", [128, NT_FULL], F32)

    W1h = sb("W1h", [128, KC * 1024], BF16)
    W1l = sb("W1l", [128, KC * 1024], BF16)
    W2h = sb("W2h", [128, KC * 1024], BF16)
    W2l = sb("W2l", [128, KC * 1024], BF16)
    wtmp = [sb(f"wtmp{i}", [128, 2048], F32) for i in range(2)]

    tmpi = sb("tmpi", [128, 256], I32)
    tmpi2 = sb("tmpi2", [128, 256], I32)

    KVsb = sb("KVsb", [128, 8 * 65], F32)
    KVpad32 = sb("KVpad32", [128, 16 * 65], F32)
    KVpadh = sb("KVpadh", [128, 16 * 65], BF16)
    KVpadl = sb("KVpadl", [128, 16 * 65], BF16)

    xt = [sb(f"xt{i}", [128, D], F32) for i in range(2)]
    xnbp = [sb(f"xnb{i}", [128, D], F32) for i in range(2)]
    FAp = [sb(f"FA{i}", [128, D], F32) for i in range(2)]
    FBp = [sb(f"FB{i}", [128, D], F32) for i in range(2)]
    FC = sb("FC", [128, D], F32)
    FD = sb("FD", [128, D], F32)
    FE = sb("FE", [128, D], F32)
    sHp = [sb(f"sH{i}", [128, D], BF16) for i in range(2)]
    sLp = [sb(f"sL{i}", [128, D], BF16) for i in range(2)]
    fH = sb("fH", [128, D], BF16)
    fL = sb("fL", [128, D], BF16)
    kHp = [sb(f"kH{i}", [128, D], BF16) for i in range(2)]
    kLp = [sb(f"kL{i}", [128, D], BF16) for i in range(2)]
    vaugHp = [sb(f"vaugH{i}", [128, 16 * 65], BF16) for i in range(2)]
    vaugLp = [sb(f"vaugL{i}", [128, 16 * 65], BF16) for i in range(2)]

    x1sb = [sb(f"x1sb{i}", [128, 2048], BF16) for i in range(2)]
    gt = [sb(f"gt{i}", [128, 256], F32) for i in range(2)]
    sfall = sb("sfall", [128, NT_OWN], F32)
    hHp = [sb(f"hH{i}", [128, 256], BF16) for i in range(2)]
    hLp = [sb(f"hL{i}", [128, 256], BF16) for i in range(2)]
    ysb = sb("ysb", [128, 1024], F32)

    epscol = sb("epscol", [128, 1], F32)
    st6 = sb("st6", [128, 12], F32)
    st2 = sb("st2", [128, 2], F32)
    mcp = [sb(f"mc{i}", [128, 1], F32) for i in range(2)]
    rcp = [sb(f"rc{i}", [128, 1], F32) for i in range(2)]
    t1c = sb("t1c", [128, 1], F32)
    t2c = sb("t2c", [128, 1], F32)
    amc = sb("amc", [128, 1], F32)
    asc = sb("asc", [128, 1], F32)
    sfc = sb("sfc", [128, 1], F32)
    zc = sb("zc", [128, 16], F32)

    pb = [ps(f"pb{i}", [128, 512], F32) for i in range(6)]
    tp = [ps(f"tp{i}", [128, 512], F32) for i in range(2)]

    P = Prog()

    VSELF = {"sem": None, "cnt": 0}

    def vdrain(e, instr):
        VSELF["cnt"] += 1
        instr.then_inc(VSELF["sem"], 1)
        e.wait_ge(VSELF["sem"], VSELF["cnt"])

    def dma_in(dst, src, bufw, bufr=(), q="sync"):
        def fn(e, dst=dst, src=src):
            return [e.dma_start(out=dst, in_=src)]
        return P.dma(q, fn, reads=bufr, writes=bufw)

    def dma_out(dst, src, bufr, bufw=(), q="gpsimd", accum=False):
        def fn(e, dst=dst, src=src, accum=accum):
            if accum:
                return [e.dma_start(out=dst, in_=src, accum_op=AL.add)]
            return [e.dma_start(out=dst, in_=src)]
        return P.dma(q, fn, reads=bufr, writes=bufw)

    def vrun(fn, reads=(), writes=()):
        return P.run("vector", fn, reads, writes)

    def srun(fn, reads=(), writes=()):
        return P.run("scalar", fn, reads, writes)

    def trun(fn, reads=(), writes=()):
        return P.run("tensor", fn, reads, writes)

    def transp(dst_sb, dst_name, src_ap, src_bufs, n_blk):
        for g in range(0, n_blk, 4):
            nb = min(4, n_blk - g)
            bank = (g // 4) % 2
            tpb = f"tp{bank}"

            def tfn(e, g=g, nb=nb, bank=bank, src_ap=src_ap):
                last = None
                for b in range(nb):
                    last = e.matmul(tp[bank][:, 128 * b: 128 * b + 128],
                                    src_ap[:, 128 * (g + b): 128 * (g + b + 1)],
                                    id16[:], start=True, stop=True)
                return last
            trun(tfn, reads=tuple(src_bufs) + ("id16",), writes=(tpb,))

            def efn(e, g=g, nb=nb, bank=bank, dst_sb=dst_sb):
                return e.copy(dst_sb[:, 128 * g: 128 * (g + nb)],
                              tp[bank][:, 0: 128 * nb])
            srun(efn, reads=(tpb,), writes=(dst_name,))

    def ln_stats(x_ap, x_bufs, mdst, rdst, mname, rname, trash_ap=None,
                 trash_name=None):
        def f1(e, x_ap=x_ap, mdst=mdst):
            e.bn_stats(st6[:, 0:6], x_ap[:, 0:512])
            i1 = e.bn_stats(st6[:, 6:12], x_ap[:, 512:1024])
            vdrain(e, i1)
            i2 = e.bn_aggr(st2[:], st6[:])
            vdrain(e, i2)
            e.tensor_copy(mdst, st2[:, 0:1])
            return e.tensor_copy(t1c[:], st2[:, 1:2])
        vrun(f1, reads=x_bufs, writes=(mname, "st6", "st2", "t1c"))

        def f4(e):
            e.activation(t2c[:], t1c[:], AF.Ln, bias=epscol[:])
            return e.activation(t2c[:], t2c[:], AF.Exp, scale=0.5)
        srun(f4, reads=("t1c", "epscol"), writes=("t2c",))

        def f5(e, rdst=rdst):
            r = e.reciprocal(rdst, t2c[:])
            vdrain(e, r)
            return e.engine_nop()
        vrun(f5, reads=("t2c",), writes=(rname,))

    def load_weight_pairs(dsrc, Wh, Wl, whname, wlname, row0, col0):
        for g in range(KC // 2):
            wb = g % 2
            src3 = dsrc[row0 + 256 * g: row0 + 256 * (g + 1),
                        col0: col0 + 1024].rearrange("(two p) f -> p two f",
                                                     two=2)
            dst3 = wtmp[wb][:].rearrange("p (two f) -> p two f", two=2)
            dma_in(dst3, src3, bufw=(f"wtmp{wb}",), q="sync")
            for cc in range(2):
                c = 2 * g + cc

                def fh(e, c=c, cc=cc, wb=wb, Wh=Wh):
                    return e.copy(Wh[:, 1024 * c: 1024 * (c + 1)],
                                  wtmp[wb][:, 1024 * cc: 1024 * (cc + 1)])
                srun(fh, reads=(f"wtmp{wb}",), writes=(f"{whname}:c{c}",))

                def fl(e, c=c, cc=cc, wb=wb, Wh=Wh, Wl=Wl):
                    return e.tensor_sub(Wl[:, 1024 * c: 1024 * (c + 1)],
                                        wtmp[wb][:, 1024 * cc: 1024 * (cc + 1)],
                                        Wh[:, 1024 * c: 1024 * (c + 1)])
                vrun(fl, reads=(f"wtmp{wb}", f"{whname}:c{c}"),
                     writes=(f"{wlname}:c{c}",))

    def stage_bias_splits(dsrc, h_ap, l_ap, hname, lname):
        dma_in(rowstage[0:1, 0:1024], dsrc[0:1, :], bufw=("rowstage",))

        def fn(e, h_ap=h_ap, l_ap=l_ap):
            i1 = e.tensor_copy(h_ap, rowstage[0:1, 0:1024])
            vdrain(e, i1)
            return e.tensor_sub(l_ap, rowstage[0:1, 0:1024], h_ap)
        vrun(fn, reads=("rowstage",), writes=(hname, lname))

    def build_rep(dsrc, idx):
        dma_in(rowstage[0:1, 0:D], dsrc[0:1, 0:D], bufw=("rowstage",))
        for hlf in range(2):
            bank = hlf % 2

            def mfn(e, hlf=hlf, bank=bank):
                return e.matmul(pb[bank][:, :], ones1[:],
                                rowstage[0:1, 512 * hlf: 512 * (hlf + 1)],
                                start=True, stop=True)
            trun(mfn, reads=("rowstage", "ones1"), writes=(f"pb{bank}",))

            def efn(e, hlf=hlf, bank=bank, idx=idx):
                return e.tensor_copy(rep[idx][:, 512 * hlf: 512 * (hlf + 1)],
                                     pb[bank][:, :])
            vrun(efn, reads=(f"pb{bank}",), writes=(f"rep{idx}",))

    def proj_tm(fh_ap, fl_ap, f_bufs, Wh, Wl, wh_name, wl_name, bias, banks=(0, 1)):
        bh, bl, bhn, bln = bias
        for hlf in range(2):
            bk = banks[hlf]
            for c in range(KC):
                def mfn(e, hlf=hlf, bk=bk, c=c, fh_ap=fh_ap, fl_ap=fl_ap,
                        Wh=Wh, Wl=Wl):
                    last = None
                    for pi, (a_ap, b_t) in enumerate(((fh_ap, Wh), (fh_ap, Wl),
                                                      (fl_ap, Wh))):
                        last = e.matmul(
                            pb[bk][:, :],
                            a_ap[:, 128 * c: 128 * (c + 1)],
                            b_t[:, 1024 * c + 512 * hlf: 1024 * c + 512 * hlf + 512],
                            start=(c == 0 and pi == 0), stop=False)
                    return last
                trun(mfn, reads=tuple(f_bufs) + (f"{wh_name}:c{c}",
                                                 f"{wl_name}:c{c}"),
                     writes=(f"pb{bk}",))

            def bfn(e, hlf=hlf, bk=bk, bh=bh, bl=bl):
                e.matmul(pb[bk][:, :], ones1b[:], bh[0:1, 512 * hlf: 512 * hlf + 512],
                         start=False, stop=False)
                return e.matmul(pb[bk][:, :], ones1b[:],
                                bl[0:1, 512 * hlf: 512 * hlf + 512],
                                start=False, stop=True)
            trun(bfn, reads=("ones1b", bhn, bln), writes=(f"pb{bk}",))

    dma_in(id16[:], did[:, :], bufw=("id16",))

    def init_const(e):
        e.memset(ones1[:], 1.0)
        e.memset(ones1b[:], 1.0)
        e.memset(epscol[:], 1e-5)
        e.memset(onescol[:], 1.0)
        e.memset(KVsb[:], 0.0)
        e.memset(KVpad32[:], 0.0)
        for p_ in range(2):
            for h in range(16):
                e.memset(vaugHp[p_][:, 65 * h + 64: 65 * h + 65], 1.0)
                e.memset(vaugLp[p_][:, 65 * h + 64: 65 * h + 65], 0.0)
        return e.engine_nop()
    vrun(init_const, writes=("ones1", "ones1b", "onescol", "KVsb", "KVpad32",
                             "vaugH0", "vaugL0", "vaugH1", "vaugL1", "epscol"))

    dma_in(rowstage[0:1, 0:1], dws[0:1, :], bufw=("rowstage",))

    def ws_mm(e):
        return e.matmul(pb[0][:, 0:1], ones1[:], rowstage[0:1, 0:1], start=True, stop=True)
    trun(ws_mm, reads=("rowstage", "ones1"), writes=("pb0",))

    def ws_ev(e):
        return e.tensor_copy(wscol[:], pb[0][:, 0:1])
    vrun(ws_ev, reads=("pb0",), writes=("wscol",))

    for bq4 in range(FF // 1024):
        dma_in(rowstage[0:1, 0:1024], db1[0:1, 1024 * bq4: 1024 * (bq4 + 1)],
               bufw=("rowstage",))

        def b1mm(e):
            last = None
            for f in range(8):
                last = e.matmul(pb[0][:, f: f + 1],
                                rowstage[0:1, 128 * f: 128 * (f + 1)],
                                onescol[:], start=True, stop=True)
            return last
        trun(b1mm, reads=("rowstage", "onescol"), writes=("pb0",))

        def b1ev(e, bq4=bq4):
            return e.tensor_copy(b1cols[:, 8 * bq4: 8 * (bq4 + 1)], pb[0][:, 0:8])
        vrun(b1ev, reads=("pb0",), writes=("b1cols",))


    load_weight_pairs(dWk, W1h, W1l, "W1h", "W1l", 0, 0)
    load_weight_pairs(dWv, W2h, W2l, "W2h", "W2l", 0, 0)
    stage_bias_splits(dbk, rowh[0:1, :], rowl[0:1, :], "rowh", "rowl")
    stage_bias_splits(dbv, rowh2[0:1, :], rowl2[0:1, :], "rowh2", "rowl2")

    def a_front(i):
        xb = i % 2
        p = i % 2
        xname = f"xt{xb}"
        xnb, sH, sL = xnbp[p], sHp[p], sLp[p]
        dma_in(xt[xb][:], xr[128 * i: 128 * (i + 1), :], bufw=(xname,))
        ln_stats(xt[xb][:], (xname,), m1s[:, i: i + 1], rs1[:, i: i + 1],
                 f"m1s{i}", f"rs1{i}")

        def xnfn(e, i=i, xb=xb, xnb=xnb, sH=sH, sL=sL):
            e.tensor_scalar(xnb[:], xt[xb][:], m1s[:, i: i + 1], rs1[:, i: i + 1],
                            AL.subtract, AL.mult)
            e.tensor_copy(sH[:], xnb[:])
            return e.tensor_sub(sL[:], xnb[:], sH[:])
        vrun(xnfn, reads=(xname, f"m1s{i}", f"rs1{i}"),
             writes=(f"xnb{p}", f"sH{p}", f"sL{p}"))

    def a_transp(i):
        p = i % 2
        transp(fH, "fH", sHp[p][:], (f"sH{p}",), KC)
        transp(fL, "fL", sLp[p][:], (f"sL{p}",), KC)

    a_front(0)
    a_transp(0)
    proj_tm(fH[:], fL[:], ("fH", "fL"), W1h, W1l, "W1h", "W1l",
            (rowh, rowl, "rowh", "rowl"), banks=(0, 1))
    for i in range(NT_FULL):
        p = i % 2
        sH, sL = sHp[p], sLp[p]
        FA, FB = FAp[p], FBp[p]
        kH, kL = kHp[p], kLp[p]
        vaugH, vaugL = vaugHp[p], vaugLp[p]
        nFA, nFB = f"FA{p}", f"FB{p}"
        proj_tm(fH[:], fL[:], ("fH", "fL"), W2h, W2l, "W2h", "W2l",
                (rowh2, rowl2, "rowh2", "rowl2"), banks=(2, 3))
        if i + 1 < NT_FULL:
            a_front(i + 1)
        for hlf in range(2):
            def kel(e, hlf=hlf, FA=FA, FB=FB):
                bap = pb[hlf][:, :]
                e.tensor_scalar(FA[:, 512 * hlf: 512 * (hlf + 1)], bap, 0.0, None, AL.min)
                return e.tensor_scalar(FB[:, 512 * hlf: 512 * (hlf + 1)], bap, 0.0,
                                       None, AL.max)
            vrun(kel, reads=(f"pb{hlf}",), writes=(nFA, nFB))

        def kexp(e, FA=FA):
            return e.activation(FE[:], FA[:], AF.Exp)
        srun(kexp, reads=(nFA,), writes=("FE",))

        def ksum(e, FB=FB, kH=kH, kL=kL):
            e.tensor_add(FC[:], FE[:], FB[:])
            e.tensor_copy(kH[:], FC[:])
            return e.tensor_sub(kL[:], FC[:], kH[:])
        vrun(ksum, reads=("FE", nFB), writes=("FC", f"kH{p}", f"kL{p}"))
        for hlf in range(2):
            def vev(e, hlf=hlf):
                return e.tensor_copy(FD[:, 512 * hlf: 512 * (hlf + 1)],
                                     pb[2 + hlf][:, :])
            vrun(vev, reads=(f"pb{2 + hlf}",), writes=("FD",))

        def vsplit(e, vaugH=vaugH, vaugL=vaugL):
            vaugH_d = vaugH[:].rearrange("p (h m) -> p h m", m=65)[:, :, 0:64]
            vaugL_d = vaugL[:].rearrange("p (h m) -> p h m", m=65)[:, :, 0:64]
            fd3 = FD[:].rearrange("p (h m) -> p h m", m=64)
            e.tensor_copy(vaugH_d, fd3)
            return e.tensor_tensor(vaugL_d, fd3, vaugH_d, AL.subtract)
        vrun(vsplit, reads=("FD",), writes=(f"vaugH{p}", f"vaugL{p}"))
        if i + 1 < NT_FULL:
            a_transp(i + 1)
            proj_tm(fH[:], fL[:], ("fH", "fL"), W1h, W1l, "W1h", "W1l",
                    (rowh, rowl, "rowh", "rowl"), banks=(0, 1))

        def kvmm(e, kH=kH, kL=kL, vaugH=vaugH, vaugL=vaugL):
            last = None
            for h in range(16):
                pair, odd = h // 2, h % 2
                bank = 4 if pair < 4 else 5
                col = 65 * (pair % 4)
                out_ap = pb[bank][64 * odd: 64 * odd + 64, col: col + 65]
                for pi, (a, bsrc) in enumerate(((kH, vaugH), (kH, vaugL), (kL, vaugH))):
                    last = e.matmul(out_ap, a[:, 64 * h: 64 * h + 64],
                                    bsrc[:, 65 * h: 65 * h + 65],
                                    start=(pi == 0), stop=(pi == 2),
                                    tile_position=(0, 64 * odd))
            return last
        trun(kvmm, reads=(f"kH{p}", f"kL{p}", f"vaugH{p}", f"vaugL{p}"),
             writes=("pb4", "pb5"))

        def kvacc(e):
            e.tensor_tensor(KVsb[:, 0:260], pb[4][:, 0:260], KVsb[:, 0:260], AL.add)
            return e.tensor_tensor(KVsb[:, 260:520], pb[5][:, 0:260],
                                   KVsb[:, 260:520], AL.add)
        vrun(kvacc, reads=("pb4", "pb5", "KVsb"), writes=("KVsb",))

    kv3 = KVsb[:].rearrange("p (q m) -> p q m", m=65)
    pad3 = KVpad32[:].rearrange("p (h m) -> p h m", m=65)
    for q_ in range(8):
        dma_out(pad3[0:64, 2 * q_: 2 * q_ + 1, :], kv3[0:64, q_: q_ + 1, :],
                bufr=("KVsb",), bufw=("KVpad32",))
        dma_out(pad3[64:128, 2 * q_ + 1: 2 * q_ + 2, :], kv3[64:128, q_: q_ + 1, :],
                bufr=("KVsb",), bufw=("KVpad32",))

    def kvsplit(e):
        e.tensor_copy(KVpadh[:], KVpad32[:])
        return e.tensor_sub(KVpadl[:], KVpad32[:], KVpadh[:])
    vrun(kvsplit, reads=("KVpad32",), writes=("KVpadh", "KVpadl"))

    load_weight_pairs(dWq, W1h, W1l, "W1h", "W1l", 0, 0)
    load_weight_pairs(dWo, W2h, W2l, "W2h", "W2l", 0, 0)
    stage_bias_splits(dbq, rowh[0:1, :], rowl[0:1, :], "rowh", "rowl")
    stage_bias_splits(dbo, rowh2[0:1, :], rowl2[0:1, :], "rowh2", "rowl2")
    build_rep(dg1, 0)
    build_rep(dbe1, 1)

    def b_front(i):
        xb = i % 2
        p = i % 2
        xname = f"xt{xb}"
        xnb, sH, sL = xnbp[p], sHp[p], sLp[p]
        dma_in(xt[xb][:], xr[128 * i: 128 * (i + 1), :], bufw=(xname,))

        def xnfn(e, i=i, xb=xb, xnb=xnb, sH=sH, sL=sL):
            e.tensor_scalar(xnb[:], xt[xb][:], m1s[:, i: i + 1], rs1[:, i: i + 1],
                            AL.subtract, AL.mult)
            e.tensor_copy(sH[:], xnb[:])
            return e.tensor_sub(sL[:], xnb[:], sH[:])
        vrun(xnfn, reads=(xname, f"m1s{i}", f"rs1{i}"),
             writes=(f"xnb{p}", f"sH{p}", f"sL{p}"))

    def b_tQ(i):
        p = i % 2
        transp(fH, "fH", sHp[p][:], (f"sH{p}",), KC)
        transp(fL, "fL", sLp[p][:], (f"sL{p}",), KC)

    def b_qproj(i):
        p = i % 2
        qb = (0, 1) if p == 0 else (4, 5)
        proj_tm(fH[:], fL[:], ("fH", "fL"), W1h, W1l, "W1h", "W1l",
                (rowh, rowl, "rowh", "rowl"), banks=qb)

    def b_elu(i):
        p = i % 2
        FA, FB = FAp[p], FBp[p]
        nFA, nFB = f"FA{p}", f"FB{p}"
        qb = (0, 1) if p == 0 else (4, 5)
        sH, sL = sHp[p], sLp[p]
        for hlf in range(2):
            def qel(e, hlf=hlf, FA=FA, FB=FB, qb=qb):
                bap = pb[qb[hlf]][:, :]
                e.tensor_scalar(FA[:, 512 * hlf: 512 * (hlf + 1)], bap, 0.0, None, AL.min)
                return e.tensor_scalar(FB[:, 512 * hlf: 512 * (hlf + 1)], bap, 0.0,
                                       None, AL.max)
            vrun(qel, reads=(f"pb{qb[hlf]}",), writes=(nFA, nFB))

        def qexp(e, FA=FA):
            return e.activation(FE[:], FA[:], AF.Exp)
        srun(qexp, reads=(nFA,), writes=("FE",))

        def qsum(e, FB=FB, sH=sH, sL=sL):
            e.tensor_add(FC[:], FE[:], FB[:])
            e.tensor_copy(sH[:], FC[:])
            return e.tensor_sub(sL[:], FC[:], sH[:])
        vrun(qsum, reads=("FE", nFB), writes=("FC", f"sH{p}", f"sL{p}"))

    def b_qk_group(i, grp):
        p = i % 2
        bank = 2 + (grp % 2)
        bname = f"pb{bank}"

        def qkmm(e, grp=grp, bank=bank):
            last = None
            for hh in range(4):
                h = 4 * grp + hh
                c = h // 2
                out_ap = pb[bank][:, 65 * hh: 65 * hh + 65]
                for pi, (a, bsrc) in enumerate(((fH, KVpadh), (fH, KVpadl),
                                                (fL, KVpadh))):
                    last = e.matmul(out_ap, a[:, 128 * c: 128 * (c + 1)],
                                    bsrc[:, 65 * h: 65 * h + 65],
                                    start=(pi == 0), stop=(pi == 2))
            return last
        trun(qkmm, reads=("fH", "fL", "KVpadh", "KVpadl"), writes=(bname,))

        def zrec(e, grp=grp, bank=bank):
            den = pb[bank][:, 64: 64 + 4 * 65].rearrange(
                "p (h m) -> p h m", m=65)[:, :, 0:1]
            zsl = zc[:, 4 * grp: 4 * grp + 4]
            i1 = e.tensor_scalar(zsl, den, 1e-6, None, AL.add)
            vdrain(e, i1)
            r = e.reciprocal(zsl, zsl)
            vdrain(e, r)
            return e.engine_nop()
        vrun(zrec, reads=(bname,), writes=("zc",))

        def osc(e, grp=grp, bank=bank):
            last = None
            for hh in range(4):
                h = 4 * grp + hh
                last = e.tensor_scalar(FD[:, 64 * h: 64 * h + 64],
                                       pb[bank][:, 65 * hh: 65 * hh + 64],
                                       zc[:, h: h + 1], None, AL.mult)
            return last
        vrun(osc, reads=(bname, "zc"), writes=("FD",))

    def b_osplit(i):
        p = i % 2
        sH, sL = sHp[p], sLp[p]

        def osplit(e, sH=sH, sL=sL):
            e.tensor_copy(sH[:], FD[:])
            return e.tensor_sub(sL[:], FD[:], sH[:])
        vrun(osplit, reads=("FD",), writes=(f"sH{p}", f"sL{p}"))

    def b_attn(i):
        p = i % 2
        qb = (0, 1) if p == 0 else (4, 5)
        proj_tm(fH[:], fL[:], ("fH", "fL"), W2h, W2l, "W2h", "W2l",
                (rowh2, rowl2, "rowh2", "rowl2"), banks=qb)

    def b_tail(i):
        p = i % 2
        qb = (0, 1) if p == 0 else (4, 5)
        FA, FB = FAp[p], FBp[p]
        nFA, nFB = f"FA{p}", f"FB{p}"
        xnb, sH, sL = xnbp[p], sHp[p], sLp[p]
        for hlf in range(2):
            def r1fn(e, hlf=hlf, FA=FA, xnb=xnb, qb=qb):
                return e.tensor_add(FA[:, 512 * hlf: 512 * (hlf + 1)],
                                    pb[qb[hlf]][:, :],
                                    xnb[:, 512 * hlf: 512 * (hlf + 1)])
            vrun(r1fn, reads=(f"pb{qb[hlf]}", f"xnb{p}"), writes=(nFA,))
        ln_stats(FA[:], (nFA,), mcp[p][:], rcp[p][:], f"mc{p}", f"rc{p}")

        def x1fn(e, FA=FA, FB=FB, sH=sH, sL=sL, p=p):
            e.tensor_scalar(FB[:], FA[:], mcp[p][:], rcp[p][:], AL.subtract, AL.mult)
            e.tensor_mul(FB[:], FB[:], rep[0][:])
            e.tensor_add(FB[:], FB[:], rep[1][:])
            e.tensor_copy(sH[:], FB[:])
            return e.tensor_sub(sL[:], FB[:], sH[:])
        vrun(x1fn, reads=(nFA, f"mc{p}", f"rc{p}", "rep0", "rep1"),
             writes=(nFB, f"sH{p}", f"sL{p}"))
        dma_out(dx1t_h[128 * i: 128 * (i + 1), :], sH[:], bufr=(f"sH{p}",),
                bufw=(f"d_x1th{i}",))
        dma_out(dx1t_l[128 * i: 128 * (i + 1), :], sL[:], bufr=(f"sL{p}",),
                bufw=(f"d_x1tl{i}",))

    def b_x1f(i):
        p = i % 2
        transp(fH, "fH", sHp[p][:], (f"sH{p}",), KC)
        transp(fL, "fL", sLp[p][:], (f"sL{p}",), KC)
        dma_out(dx1f_h[i, :, :], fH[:], bufr=("fH",), bufw=(f"d_x1fh{i}",))
        dma_out(dx1f_l[i, :, :], fL[:], bufr=("fL",), bufw=(f"d_x1fl{i}",))

    b_front(0)
    b_front(1)
    for j in range(NT_OWN // 2):
        a_, b_ = 2 * j, 2 * j + 1
        b_tQ(a_)
        b_qproj(a_)
        b_tQ(b_)
        b_elu(a_)
        b_qproj(b_)
        b_tQ(a_)
        b_elu(b_)
        for grp in range(4):
            b_qk_group(a_, grp)
        b_osplit(a_)
        b_tQ(b_)
        for grp in range(4):
            b_qk_group(b_, grp)
        b_tQ(a_)
        b_attn(a_)
        b_osplit(b_)
        b_tQ(b_)
        b_tail(a_)
        b_attn(b_)
        b_x1f(a_)
        b_tail(b_)
        if a_ + 2 < NT_OWN:
            b_front(a_ + 2)
        b_x1f(b_)
        if b_ + 2 < NT_OWN:
            b_front(b_ + 2)

    build_rep(db2, 2)
    build_rep(dg2, 0)
    build_rep(dbe2, 1)

    def d1_tile_thunks(i):
        th = []
        xb = i % 2
        p = i % 2
        xname = f"xt{xb}"
        FA, FB = FAp[p], FBp[p]
        nFA, nFB = f"FA{p}", f"FB{p}"
        sH, kH, kL = sHp[p], kHp[p], kLp[p]
        def t_loads():
            dma_in(xt[xb][:], xr[128 * i: 128 * (i + 1), :], bufw=(xname,))
            dma_in(kH[:], dx1t_h[128 * i: 128 * (i + 1), :], bufw=(f"kH{p}",),
                   bufr=(f"d_x1th{i}",))
            dma_in(kL[:], dx1t_l[128 * i: 128 * (i + 1), :], bufw=(f"kL{p}",),
                   bufr=(f"d_x1tl{i}",))
            dma_in(FE[:], dyacc[128 * i: 128 * (i + 1), :], bufw=("FE",),
                   bufr=(f"d_yacc{i}",))
        th.append(t_loads)

        def t_r2(FA=FA, kH=kH, kL=kL):
            def r2fn(e, FA=FA, kH=kH, kL=kL):
                e.tensor_add(FA[:], kH[:], kL[:])
                return e.tensor_add(FA[:], FA[:], FE[:])
            vrun(r2fn, reads=(f"kH{p}", f"kL{p}", "FE"), writes=(nFA,))
        th.append(t_r2)
        th.append(lambda: ln_stats(FA[:], (nFA,), mcp[p][:], rcp[p][:],
                                   f"mc{p}", f"rc{p}"))

        def t_xmid():
            def xmid(e, xb=xb, FA=FA, FB=FB, p=p):
                e.tensor_scalar(FB[:], FA[:], mcp[p][:], rcp[p][:], AL.subtract,
                                AL.mult)
                e.tensor_mul(FB[:], FB[:], rep[0][:])
                e.tensor_add(FB[:], FB[:], rep[1][:])
                return e.tensor_add(FB[:], FB[:], xt[xb][:])
            vrun(xmid, reads=(nFA, f"mc{p}", f"rc{p}", "rep0", "rep1", xname),
                 writes=(nFB,))
            dma_out(dxmid[128 * i: 128 * (i + 1), :], FB[:], bufr=(nFB,),
                    bufw=(f"d_xmid{'e' if i % 2 == 0 else 'o'}{i}",))
        th.append(t_xmid)
        th.append(lambda: ln_stats(FB[:], (nFB,), mcp[p][:], rcp[p][:],
                                   f"mc{p}", f"rc{p}"))

        def t_qgen():
            def qgen(e, FB=FB, p=p, i=i):
                e.tensor_scalar(FC[:], FB[:], mcp[p][:], rcp[p][:], AL.subtract,
                                AL.mult)
                i1 = e.reduce_max(amc[:], FC[:], axis=AX.X,
                                  apply_absolute_value=True)
                vdrain(e, i1)
                i2 = e.tensor_scalar(amc[:], amc[:], 1e-5, None, AL.max)
                vdrain(e, i2)
                return e.scalar_tensor_tensor(sfall[:, i: i + 1], amc[:],
                                              1.0 / 127.0, wscol[:],
                                              AL.mult, AL.mult)
            vrun(qgen, reads=(nFB, f"mc{p}", f"rc{p}", "wscol"),
                 writes=("FC", "amc", "sfall"))

            def asfn(e):
                i1 = e.reciprocal(asc[:], amc[:])
                vdrain(e, i1)
                r = e.tensor_scalar(asc[:], asc[:], 127.0, None, AL.mult)
                vdrain(e, r)
                return e.engine_nop()
            vrun(asfn, reads=("amc",), writes=("asc",))
        th.append(t_qgen)

        def t_lad0():
            def lad0(e):
                e.tensor_scalar(FD[:], FC[:], asc[:], 128.5, AL.mult, AL.add)
                return e.tensor_copy(FE[:], FD[:])
            vrun(lad0, reads=("FC", "asc"), writes=("FD", "FE"))
        th.append(t_lad0)

        def t_lad1():
            def ladder_a(e):
                for k in range(7, 3, -1):
                    e.tensor_scalar(FC[:], FE[:], float(2 ** k), float(-(2 ** k)),
                                    AL.is_ge, AL.mult)
                    e.tensor_add(FE[:], FE[:], FC[:])
                return e.engine_nop()
            vrun(ladder_a, reads=("FE",), writes=("FE", "FC"))
        th.append(t_lad1)

        def t_lad2(sH=sH):
            def ladder_b(e, sH=sH):
                for k in range(3, -1, -1):
                    e.tensor_scalar(FC[:], FE[:], float(2 ** k), float(-(2 ** k)),
                                    AL.is_ge, AL.mult)
                    e.tensor_add(FE[:], FE[:], FC[:])
                e.tensor_sub(FD[:], FD[:], FE[:])
                return e.tensor_scalar(sH[:], FD[:], -128.0, None, AL.add)
            vrun(ladder_b, reads=("FD", "FE"), writes=("FD", "FE", "FC", f"sH{p}"))
            dma_out(dqt[128 * i: 128 * (i + 1), :], sH[:], bufr=(f"sH{p}",),
                    bufw=(f"d_qt{'e' if i % 2 == 0 else 'o'}{i}",))
        th.append(t_lad2)
        return th

    NE = NQ
    d1pend = []
    for qr in range(NQ):
        load_weight_pairs(dW1, W1h, W1l, "W1h", "W1l", 0, 1024 * qr)
        load_weight_pairs(dW2, W2h, W2l, "W2h", "W2l", 1024 * qr, 0)
        for sc_ in range(OWN // 256):
            bh, bl = "x1sb0", "x1sb1"
            sbh, sbl = x1sb[0], x1sb[1]
            dma_in(sbh[:, 0:1024], dx1f_h[2 * sc_, :, :], bufw=(bh,),
                   bufr=(f"d_x1fh{2 * sc_}",))
            dma_in(sbh[:, 1024:2048], dx1f_h[2 * sc_ + 1, :, :], bufw=(bh,),
                   bufr=(f"d_x1fh{2 * sc_ + 1}",))
            dma_in(sbl[:, 0:1024], dx1f_l[2 * sc_, :, :], bufw=(bl,),
                   bufr=(f"d_x1fl{2 * sc_}",))
            dma_in(sbl[:, 1024:2048], dx1f_l[2 * sc_ + 1, :, :], bufw=(bl,),
                   bufr=(f"d_x1fl{2 * sc_ + 1}",))
            rh3 = sbh[:].rearrange("p (s c) -> p s c", c=1024)
            rl3 = sbl[:].rearrange("p (s c) -> p s c", c=1024)
            hbank = [(pb[4], "pb4"), (pb[5], "pb5"), (tp[0], "tp0"), (tp[1], "tp1")]

            def emit_hmm(ffc):
                hb_ap, hbn = hbank[ffc % 4]
                for c in range(KC):
                    def hmm(e, ffc=ffc, hb_ap=hb_ap, rh3=rh3, rl3=rl3, c=c):
                        last = None
                        base = 1024 * c + 128 * ffc
                        for pi, (wt, xs3) in enumerate(((W1h, rh3), (W1l, rh3),
                                                        (W1h, rl3))):
                            last = e.matmul(
                                hb_ap[:, 0:256],
                                wt[:, base: base + 128],
                                xs3[:, :, 128 * c: 128 * (c + 1)],
                                start=(c == 0 and pi == 0),
                                stop=(c == KC - 1 and pi == 2))
                        return last
                    trun(hmm, reads=(bh, bl, f"W1h:c{c}", f"W1l:c{c}"),
                         writes=(hbn,))

            for f0 in range(4):
                emit_hmm(f0)
            for ffc in range(8):
                if d1pend:
                    d1pend.pop(0)()
                hb_ap, hbn = hbank[ffc % 4]
                fp = ffc % 2
                hH, hL = hHp[fp], hLp[fp]
                gtmp = gt[fp]
                gname = f"gt{fp}"

                def gel(e, ffc=ffc, hb_ap=hb_ap, qr=qr, gtmp=gtmp):
                    return e.activation(gtmp[:, 0:256], hb_ap[:, 0:256], AF.Gelu,
                                        bias=b1cols[:, 8 * qr + ffc: 8 * qr + ffc + 1])
                srun(gel, reads=(hbn, "b1cols"), writes=(gname,))

                def hsp(e, hH=hH, hL=hL, gtmp=gtmp):
                    e.tensor_copy(hH[:], gtmp[:, 0:256])
                    return e.tensor_sub(hL[:], gtmp[:, 0:256], hH[:])
                vrun(hsp, reads=(gname,), writes=(f"hH{fp}", f"hL{fp}"))

                if ffc + 4 < 8:
                    emit_hmm(ffc + 4)

                def ymm(e, ffc=ffc, hH=hH, hL=hL):
                    last = None
                    base0 = 1024 * ffc
                    for sub in range(2):
                        for hlf in range(2):
                            yb = 2 * sub + hlf
                            for a, wt in ((hH, W2h), (hH, W2l), (hL, W2h)):
                                last = e.matmul(
                                    pb[yb][:, :],
                                    a[:, 128 * sub: 128 * (sub + 1)],
                                    wt[:, base0 + 512 * hlf: base0 + 512 * hlf + 512],
                                    start=(ffc == 0 and a is hH and wt is W2h),
                                    stop=(ffc == 7 and a is hL))
                    return last
                trun(ymm, reads=(f"hH{fp}", f"hL{fp}", f"W2h:c{ffc}",
                                 f"W2l:c{ffc}"),
                     writes=("pb0", "pb1", "pb2", "pb3"))
            for sub in range(2):
                for hlf in range(2):
                    yb = 2 * sub + hlf

                    def yev(e, hlf=hlf, yb=yb, qr=qr):
                        dst = ysb[:, 512 * hlf: 512 * hlf + 512]
                        if qr == 0:
                            return e.tensor_add(dst, pb[yb][:, :],
                                                rep[2][:, 512 * hlf: 512 * hlf + 512])
                        return e.tensor_copy(dst, pb[yb][:, :])
                    vrun(yev, reads=((f"pb{yb}", "rep2") if qr == 0
                                     else (f"pb{yb}",)), writes=("ysb",))
                dma_out(dyacc[256 * sc_ + 128 * sub: 256 * sc_ + 128 * (sub + 1), :],
                        ysb[:],
                        bufr=("ysb",), bufw=(f"d_yacc{2 * sc_ + sub}",),
                        accum=(qr > 0))
            if qr == NE - 1:
                d1pend.extend(d1_tile_thunks(2 * sc_))

    build_rep(dbf, 2)
    for c in range(KC):
        dma_in(tmpi[:], dPT[128 * c: 128 * (c + 1), :], bufw=("tmpi",))
        for s4 in range(4):
            def ufn(e, c=c, s4=s4):
                e.tensor_scalar(tmpi2[:], tmpi[:], 2 * s4, 3,
                                AL.logical_shift_right, AL.bitwise_and)
                return e.tensor_scalar(
                    W1h[:, 1024 * c + 256 * s4: 1024 * c + 256 * (s4 + 1)],
                    tmpi2[:], -1.0, None, AL.add)
            vrun(ufn, reads=("tmpi",), writes=(f"W1h:c{c}", "tmpi2"))
        if d1pend:
            d1pend.pop(0)()
    def d2_tile(i):
        p = i % 2
        FA, FB = FAp[p], FBp[p]
        nFA, nFB = f"FA{p}", f"FB{p}"
        sL, fHx = sLp[p], (fH if p == 0 else fL)
        tb = (0, 1) if p == 0 else (4, 5)
        dma_in(sL[:], dqt[128 * i: 128 * (i + 1), :], bufw=(f"sL{p}",),
               bufr=(f"d_qt{'e' if i % 2 == 0 else 'o'}{i}",))
        dma_in(FB[:], dxmid[128 * i: 128 * (i + 1), :], bufw=(nFB,),
               bufr=(f"d_xmid{'e' if i % 2 == 0 else 'o'}{i}",))
        transp(fHx, ("fH" if p == 0 else "fL"), sL[:], (f"sL{p}",), KC)

        def tmm(e, tb=tb, fHx=fHx):
            last = None
            for hlf in range(2):
                first = True
                for c in range(KC):
                    last = e.matmul(pb[tb[hlf]][:, :], fHx[:, 128 * c: 128 * (c + 1)],
                                    W1h[:, 1024 * c + 512 * hlf: 1024 * c + 512 * hlf + 512],
                                    start=first, stop=(c == KC - 1))
                    first = False
            return last
        trun(tmm, reads=(("fH" if p == 0 else "fL"),)
             + tuple(f"W1h:c{c}" for c in range(KC)),
             writes=(f"pb{tb[0]}", f"pb{tb[1]}"))

        def dfin(e, FA=FA, FB=FB, tb=tb, i=i):
            last = None
            for hlf in range(2):
                sl = slice(512 * hlf, 512 * (hlf + 1))
                e.scalar_tensor_tensor(FA[:, sl], pb[tb[hlf]][:, :],
                                       sfall[:, i: i + 1],
                                       rep[2][:, sl], AL.mult, AL.add)
                last = e.tensor_add(FA[:, sl], FA[:, sl], FB[:, sl])
            return last
        vrun(dfin, reads=(f"pb{tb[0]}", f"pb{tb[1]}", "sfall", "rep2", nFB),
             writes=(nFA,))
        dma_out(dout[128 * i: 128 * (i + 1), :], FA[:], bufr=(nFA,), bufw=("dout",))

    while d1pend:
        d1pend.pop(0)()
    pend = []
    for j in range(NT_OWN // 2):
        d2_tile(2 * j)
        pend.extend(d1_tile_thunks(2 * j + 1))
        for _ in range(4):
            if pend:
                pend.pop(0)()
    while pend:
        pend.pop(0)()
    for j in range(NT_OWN // 2):
        d2_tile(2 * j + 1)

    final_out = P.dsem.get("D:dout", 0)

    semstack = ExitStack()
    sems = {}
    for name in ("tensor", "vector", "scalar", "gpsimd"):
        sems[name] = semstack.enter_context(nc.semaphore("sem_" + name))
    for key in P.dsem:
        sems[key] = semstack.enter_context(
            nc.semaphore("sem_" + key.replace(":", "_")))
    sems["vself"] = semstack.enter_context(nc.semaphore("sem_vself"))
    VSELF["sem"] = sems["vself"]

    with nc.Block() as block:
        def emit(eng_obj, name):
            for idx, (engine, fn, waits, kind) in enumerate(P.steps):
                if engine != name:
                    continue
                for sname, val in waits:
                    h = sems[sname]
                    v = val * 16 if sname.startswith("D:") else val
                    eng_obj.wait_ge(h, v)
                ret = fn(eng_obj)
                if kind == "eng":
                    ret.then_inc(sems[name], 1)
                else:
                    key = P.step_dkey[idx]
                    for ins in ret:
                        ins.then_inc(sems[key], 16)

        @block.sync
        def _(sync):
            emit(sync, "sync")

        @block.vector
        def _(vector):
            emit(vector, "vector")

        @block.scalar
        def _(scalar):
            emit(scalar, "scalar")

        @block.tensor
        def _(tensor):
            emit(tensor, "tensor")

        @block.gpsimd
        def _(gpsimd):
            emit(gpsimd, "gpsimd")
            gpsimd.wait_ge(sems["D:dout"], final_out * 16)

    semstack.close()
    es.close()
    return nc


def kernel(L0_attn_src, L0_attn_Wq, L0_attn_bq, L0_attn_Wk, L0_attn_bk,
           L0_attn_Wv, L0_attn_bv, L0_attn_Wo, L0_attn_bo, L0_attn_W1,
           L0_attn_b1, L0_attn_W2, L0_attn_b2, L0_attn_g1, L0_attn_be1,
           L0_attn_g2, L0_attn_be2, L0_ffn_packed_weight, L0_ffn_weight_scale,
           L0_ffn_bias):
    x = np.ascontiguousarray(np.asarray(L0_attn_src, dtype=np.float32))
    PT = np.ascontiguousarray(
        np.asarray(L0_ffn_packed_weight, dtype=np.int32).reshape(256, 1024).T)
    row = lambda a, n: np.ascontiguousarray(np.asarray(a, np.float32).reshape(1, n))
    common = {
        "Wq": np.ascontiguousarray(np.asarray(L0_attn_Wq, np.float32)),
        "Wk": np.ascontiguousarray(np.asarray(L0_attn_Wk, np.float32)),
        "Wv": np.ascontiguousarray(np.asarray(L0_attn_Wv, np.float32)),
        "Wo": np.ascontiguousarray(np.asarray(L0_attn_Wo, np.float32)),
        "W1": np.ascontiguousarray(np.asarray(L0_attn_W1, np.float32)),
        "W2": np.ascontiguousarray(np.asarray(L0_attn_W2, np.float32)),
        "bq": row(L0_attn_bq, D), "bk": row(L0_attn_bk, D),
        "bv": row(L0_attn_bv, D), "bo": row(L0_attn_bo, D),
        "b1": row(L0_attn_b1, FF), "b2": row(L0_attn_b2, D),
        "g1": row(L0_attn_g1, D), "be1": row(L0_attn_be1, D),
        "g2": row(L0_attn_g2, D), "be2": row(L0_attn_be2, D),
        "PT": PT,
        "ws": np.asarray(L0_ffn_weight_scale, np.float32).reshape(1, 1),
        "bf": row(L0_ffn_bias, D),
        "id16": np.eye(128).astype(ml_dtypes.bfloat16),
    }
    in_maps = []
    for core in range(8):
        b_, j = core // 2, core % 2
        own = x[b_, j * OWN: (j + 1) * OWN]
        other = x[b_, (1 - j) * OWN: (2 - j) * OWN]
        m = dict(common)
        m["xr"] = np.ascontiguousarray(np.concatenate([own, other], axis=0))
        in_maps.append(m)

    nc = build_nc()
    res = run_bass_kernel_spmd(nc, in_maps, core_ids=list(range(8)))
    out = np.empty((B, S, D), dtype=np.float32)
    for core in range(8):
        b_, j = core // 2, core % 2
        out[b_, j * OWN: (j + 1) * OWN] = res.results[core]["out"]
    return out


# version 32
# speedup vs baseline: 1.0468x; 1.0468x over previous
import numpy as np
import ml_dtypes
import concourse.bass as bass
import concourse.mybir as mybir
from concourse.bass_utils import run_bass_kernel_spmd
from contextlib import ExitStack

F32 = mybir.dt.float32
BF16 = mybir.dt.bfloat16
I32 = mybir.dt.int32
AF = mybir.ActivationFunctionType
AL = mybir.AluOpType
AX = mybir.AxisListType

B, S, D, H, DH, FF = 4, 4096, 1024, 16, 64, 4096
OWN = S // 2
FULL = S
NT_FULL = FULL // 128
NT_OWN = OWN // 128
KC = D // 128
NQ = 4

ENG = ("sync", "vector", "scalar", "tensor", "gpsimd")


class Prog:

    def __init__(self):
        self.steps = []
        self.step_dkey = []
        self.n = {e: 0 for e in ENG}
        self.dsem = {}
        self.w = {}
        self.r = {}

    def _sched(self, eng, fn, reads, writes, kind):
        waits = {}

        def need(tok):
            if tok is None:
                return
            sem, val = tok
            if kind == "eng" and sem == eng:
                return
            if sem.startswith("D:"):
                val = self.dsem[sem]
            if waits.get(sem, -1) < val:
                waits[sem] = val

        for b in reads:
            need(self.w.get(b))
        for b in writes:
            need(self.w.get(b))
            for t in self.r.get(b, ()):
                need(t)
        if kind == "eng":
            self.n[eng] += 1
            tok = (eng, self.n[eng])
            self.step_dkey.append(None)
        else:
            key = "D:" + self._pool(writes[0])
            self.dsem[key] = self.dsem.get(key, 0) + 1
            tok = (key, self.dsem[key])
            self.step_dkey.append(key)
        for b in reads:
            self.r.setdefault(b, []).append(tok)
        for b in writes:
            self.w[b] = tok
            self.r[b] = []
        self.steps.append((eng, fn, tuple(waits.items()), kind))
        return tok

    @staticmethod
    def _pool(buf):
        if buf.startswith("d_"):
            return buf.rstrip("0123456789")
        return buf

    def run(self, eng, fn, reads=(), writes=()):
        return self._sched(eng, fn, reads, writes, "eng")

    def dma(self, q, fn, reads=(), writes=(), n=1):
        assert writes, "dma step needs a destination buffer name"
        return self._sched(q, fn, reads, writes, "dma")


def build_nc():
    nc = bass.Bass("TRN2", target_bir_lowering=False, detect_race_conditions=False)

    xr = nc.dram_tensor("xr", [FULL, D], F32, kind="ExternalInput").ap()
    dWq = nc.dram_tensor("Wq", [D, D], F32, kind="ExternalInput").ap()
    dWk = nc.dram_tensor("Wk", [D, D], F32, kind="ExternalInput").ap()
    dWv = nc.dram_tensor("Wv", [D, D], F32, kind="ExternalInput").ap()
    dWo = nc.dram_tensor("Wo", [D, D], F32, kind="ExternalInput").ap()
    dW1 = nc.dram_tensor("W1", [D, FF], F32, kind="ExternalInput").ap()
    dW2 = nc.dram_tensor("W2", [FF, D], F32, kind="ExternalInput").ap()
    dbq = nc.dram_tensor("bq", [1, D], F32, kind="ExternalInput").ap()
    dbk = nc.dram_tensor("bk", [1, D], F32, kind="ExternalInput").ap()
    dbv = nc.dram_tensor("bv", [1, D], F32, kind="ExternalInput").ap()
    dbo = nc.dram_tensor("bo", [1, D], F32, kind="ExternalInput").ap()
    db1 = nc.dram_tensor("b1", [1, FF], F32, kind="ExternalInput").ap()
    db2 = nc.dram_tensor("b2", [1, D], F32, kind="ExternalInput").ap()
    dg1 = nc.dram_tensor("g1", [1, D], F32, kind="ExternalInput").ap()
    dbe1 = nc.dram_tensor("be1", [1, D], F32, kind="ExternalInput").ap()
    dg2 = nc.dram_tensor("g2", [1, D], F32, kind="ExternalInput").ap()
    dbe2 = nc.dram_tensor("be2", [1, D], F32, kind="ExternalInput").ap()
    dPT = nc.dram_tensor("PT", [D, 256], I32, kind="ExternalInput").ap()
    dws = nc.dram_tensor("ws", [1, 1], F32, kind="ExternalInput").ap()
    dbf = nc.dram_tensor("bf", [1, D], F32, kind="ExternalInput").ap()
    did = nc.dram_tensor("id16", [128, 128], BF16, kind="ExternalInput").ap()

    dx1t_h = nc.dram_tensor("x1t_h", [OWN, D], BF16, kind="Internal").ap()
    dx1t_l = nc.dram_tensor("x1t_l", [OWN, D], BF16, kind="Internal").ap()
    dx1f_h = nc.dram_tensor("x1f_h", [NT_OWN, 128, D], BF16, kind="Internal").ap()
    dx1f_l = nc.dram_tensor("x1f_l", [NT_OWN, 128, D], BF16, kind="Internal").ap()
    dyacc = nc.dram_tensor("yacc", [OWN, D], F32, kind="Internal").ap()
    dqt = nc.dram_tensor("dqt", [OWN, D], BF16, kind="Internal").ap()
    dxmid = nc.dram_tensor("dxmid", [OWN, D], F32, kind="Internal").ap()

    dout = nc.dram_tensor("out", [OWN, D], F32, kind="ExternalOutput").ap()

    es = ExitStack()
    sb = lambda name, shape, dt: es.enter_context(nc.sbuf_tensor(name, shape, dt))
    ps = lambda name, shape, dt: es.enter_context(nc.psum_tensor(name, shape, dt))

    id16 = sb("id16s", [128, 128], BF16)
    ones1 = sb("ones1", [1, 128], F32)
    ones1b = sb("ones1b", [1, 128], BF16)
    onescol = sb("onescol", [1, 1], F32)
    rowstage = sb("rowstage", [1, 1024], F32)
    rowh = sb("rowh", [1, 1024], BF16)
    rowl = sb("rowl", [1, 1024], BF16)
    rowh2 = sb("rowh2", [1, 1024], BF16)
    rowl2 = sb("rowl2", [1, 1024], BF16)
    rep = [sb(f"rep{i}", [128, D], F32) for i in range(3)]
    b1cols = sb("b1cols", [128, FF // 128], F32)
    wscol = sb("wscol", [128, 1], F32)
    m1s = sb("m1s", [128, NT_FULL], F32)
    rs1 = sb("rs1", [128, NT_FULL], F32)

    W1h = sb("W1h", [128, KC * 1024], BF16)
    W1l = sb("W1l", [128, KC * 1024], BF16)
    W2h = sb("W2h", [128, KC * 1024], BF16)
    W2l = sb("W2l", [128, KC * 1024], BF16)
    wtmp = [sb(f"wtmp{i}", [128, 1024], F32) for i in range(4)]

    tmpi = sb("tmpi", [128, 256], I32)
    tmpi2 = sb("tmpi2", [128, 256], I32)

    KVsb = sb("KVsb", [128, 8 * 65], F32)
    KVpad32 = sb("KVpad32", [128, 16 * 65], F32)
    KVpadh = sb("KVpadh", [128, 16 * 65], BF16)
    KVpadl = sb("KVpadl", [128, 16 * 65], BF16)

    xt = [sb(f"xt{i}", [128, D], F32) for i in range(2)]
    xnbp = [sb(f"xnb{i}", [128, D], F32) for i in range(2)]
    FAp = [sb(f"FA{i}", [128, D], F32) for i in range(2)]
    FBp = [sb(f"FB{i}", [128, D], F32) for i in range(2)]
    FC = sb("FC", [128, D], F32)
    FD = sb("FD", [128, D], F32)
    FE = sb("FE", [128, D], F32)
    sHp = [sb(f"sH{i}", [128, D], BF16) for i in range(2)]
    sLp = [sb(f"sL{i}", [128, D], BF16) for i in range(2)]
    fH = sb("fH", [128, D], BF16)
    fL = sb("fL", [128, D], BF16)
    kHp = [sb(f"kH{i}", [128, D], BF16) for i in range(2)]
    kLp = [sb(f"kL{i}", [128, D], BF16) for i in range(2)]
    vaugHp = [sb(f"vaugH{i}", [128, 16 * 65], BF16) for i in range(2)]
    vaugLp = [sb(f"vaugL{i}", [128, 16 * 65], BF16) for i in range(2)]

    x1sb = [sb(f"x1sb{i}", [128, 2048], BF16) for i in range(2)]
    gt = [sb(f"gt{i}", [128, 256], F32) for i in range(2)]
    sfall = sb("sfall", [128, NT_OWN], F32)
    hHp = [sb(f"hH{i}", [128, 256], BF16) for i in range(2)]
    hLp = [sb(f"hL{i}", [128, 256], BF16) for i in range(2)]
    ysb = sb("ysb", [128, 1024], F32)

    epscol = sb("epscol", [128, 1], F32)
    st6 = sb("st6", [128, 12], F32)
    st2 = sb("st2", [128, 2], F32)
    mcp = [sb(f"mc{i}", [128, 1], F32) for i in range(2)]
    rcp = [sb(f"rc{i}", [128, 1], F32) for i in range(2)]
    t1c = sb("t1c", [128, 1], F32)
    t2c = sb("t2c", [128, 1], F32)
    amc = sb("amc", [128, 1], F32)
    asc = sb("asc", [128, 1], F32)
    sfc = sb("sfc", [128, 1], F32)
    zc = sb("zc", [128, 16], F32)

    pb = [ps(f"pb{i}", [128, 512], F32) for i in range(6)]
    tp = [ps(f"tp{i}", [128, 512], F32) for i in range(2)]

    P = Prog()

    VSELF = {"sem": None, "cnt": 0}

    def vdrain(e, instr):
        VSELF["cnt"] += 1
        instr.then_inc(VSELF["sem"], 1)
        e.wait_ge(VSELF["sem"], VSELF["cnt"])

    def dma_in(dst, src, bufw, bufr=(), q="sync"):
        def fn(e, dst=dst, src=src):
            return [e.dma_start(out=dst, in_=src)]
        return P.dma(q, fn, reads=bufr, writes=bufw)

    def dma_out(dst, src, bufr, bufw=(), q="gpsimd", accum=False):
        def fn(e, dst=dst, src=src, accum=accum):
            if accum:
                return [e.dma_start(out=dst, in_=src, accum_op=AL.add)]
            return [e.dma_start(out=dst, in_=src)]
        return P.dma(q, fn, reads=bufr, writes=bufw)

    def vrun(fn, reads=(), writes=()):
        return P.run("vector", fn, reads, writes)

    def srun(fn, reads=(), writes=()):
        return P.run("scalar", fn, reads, writes)

    def trun(fn, reads=(), writes=()):
        return P.run("tensor", fn, reads, writes)

    def transp(dst_sb, dst_name, src_ap, src_bufs, n_blk):
        for g in range(0, n_blk, 4):
            nb = min(4, n_blk - g)
            bank = (g // 4) % 2
            tpb = f"tp{bank}"

            def tfn(e, g=g, nb=nb, bank=bank, src_ap=src_ap):
                last = None
                for b in range(nb):
                    last = e.matmul(tp[bank][:, 128 * b: 128 * b + 128],
                                    src_ap[:, 128 * (g + b): 128 * (g + b + 1)],
                                    id16[:], start=True, stop=True)
                return last
            trun(tfn, reads=tuple(src_bufs) + ("id16",), writes=(tpb,))

            def efn(e, g=g, nb=nb, bank=bank, dst_sb=dst_sb):
                return e.copy(dst_sb[:, 128 * g: 128 * (g + nb)],
                              tp[bank][:, 0: 128 * nb])
            srun(efn, reads=(tpb,), writes=(dst_name,))

    def ln_stats(x_ap, x_bufs, mdst, rdst, mname, rname, trash_ap=None,
                 trash_name=None):
        def f1(e, x_ap=x_ap, mdst=mdst):
            e.bn_stats(st6[:, 0:6], x_ap[:, 0:512])
            i1 = e.bn_stats(st6[:, 6:12], x_ap[:, 512:1024])
            vdrain(e, i1)
            i2 = e.bn_aggr(st2[:], st6[:])
            vdrain(e, i2)
            e.tensor_copy(mdst, st2[:, 0:1])
            return e.tensor_copy(t1c[:], st2[:, 1:2])
        vrun(f1, reads=x_bufs, writes=(mname, "st6", "st2", "t1c"))

        def f4(e):
            e.activation(t2c[:], t1c[:], AF.Ln, bias=epscol[:])
            return e.activation(t2c[:], t2c[:], AF.Exp, scale=0.5)
        srun(f4, reads=("t1c", "epscol"), writes=("t2c",))

        def f5(e, rdst=rdst):
            r = e.reciprocal(rdst, t2c[:])
            vdrain(e, r)
            return e.engine_nop()
        vrun(f5, reads=("t2c",), writes=(rname,))

    def load_weight_pairs(dsrc, Wh, Wl, whname, wlname, row0, col0):
        for c in range(KC):
            wb = c % 4
            dma_in(wtmp[wb][:],
                   dsrc[row0 + 128 * c: row0 + 128 * (c + 1), col0: col0 + 1024],
                   bufw=(f"wtmp{wb}",), q="sync")

            def fh(e, c=c, wb=wb, Wh=Wh):
                return e.copy(Wh[:, 1024 * c: 1024 * (c + 1)], wtmp[wb][:])
            srun(fh, reads=(f"wtmp{wb}",), writes=(f"{whname}:c{c}",))

            def fl(e, c=c, wb=wb, Wh=Wh, Wl=Wl):
                return e.tensor_sub(Wl[:, 1024 * c: 1024 * (c + 1)], wtmp[wb][:],
                                    Wh[:, 1024 * c: 1024 * (c + 1)])
            vrun(fl, reads=(f"wtmp{wb}", f"{whname}:c{c}"),
                 writes=(f"{wlname}:c{c}",))

    def stage_bias_splits(dsrc, h_ap, l_ap, hname, lname):
        dma_in(rowstage[0:1, 0:1024], dsrc[0:1, :], bufw=("rowstage",))

        def fn(e, h_ap=h_ap, l_ap=l_ap):
            i1 = e.tensor_copy(h_ap, rowstage[0:1, 0:1024])
            vdrain(e, i1)
            return e.tensor_sub(l_ap, rowstage[0:1, 0:1024], h_ap)
        vrun(fn, reads=("rowstage",), writes=(hname, lname))

    def build_rep(dsrc, idx):
        dma_in(rowstage[0:1, 0:D], dsrc[0:1, 0:D], bufw=("rowstage",))
        for hlf in range(2):
            bank = hlf % 2

            def mfn(e, hlf=hlf, bank=bank):
                return e.matmul(pb[bank][:, :], ones1[:],
                                rowstage[0:1, 512 * hlf: 512 * (hlf + 1)],
                                start=True, stop=True)
            trun(mfn, reads=("rowstage", "ones1"), writes=(f"pb{bank}",))

            def efn(e, hlf=hlf, bank=bank, idx=idx):
                return e.tensor_copy(rep[idx][:, 512 * hlf: 512 * (hlf + 1)],
                                     pb[bank][:, :])
            vrun(efn, reads=(f"pb{bank}",), writes=(f"rep{idx}",))

    def proj_tm(fh_ap, fl_ap, f_bufs, Wh, Wl, wh_name, wl_name, bias, banks=(0, 1)):
        bh, bl, bhn, bln = bias
        for hlf in range(2):
            bk = banks[hlf]
            for c in range(KC):
                def mfn(e, hlf=hlf, bk=bk, c=c, fh_ap=fh_ap, fl_ap=fl_ap,
                        Wh=Wh, Wl=Wl):
                    last = None
                    for pi, (a_ap, b_t) in enumerate(((fh_ap, Wh), (fh_ap, Wl),
                                                      (fl_ap, Wh))):
                        last = e.matmul(
                            pb[bk][:, :],
                            a_ap[:, 128 * c: 128 * (c + 1)],
                            b_t[:, 1024 * c + 512 * hlf: 1024 * c + 512 * hlf + 512],
                            start=(c == 0 and pi == 0), stop=False)
                    return last
                trun(mfn, reads=tuple(f_bufs) + (f"{wh_name}:c{c}",
                                                 f"{wl_name}:c{c}"),
                     writes=(f"pb{bk}",))

            def bfn(e, hlf=hlf, bk=bk, bh=bh, bl=bl):
                e.matmul(pb[bk][:, :], ones1b[:], bh[0:1, 512 * hlf: 512 * hlf + 512],
                         start=False, stop=False)
                return e.matmul(pb[bk][:, :], ones1b[:],
                                bl[0:1, 512 * hlf: 512 * hlf + 512],
                                start=False, stop=True)
            trun(bfn, reads=("ones1b", bhn, bln), writes=(f"pb{bk}",))

    dma_in(id16[:], did[:, :], bufw=("id16",))

    def init_const(e):
        e.memset(ones1[:], 1.0)
        e.memset(ones1b[:], 1.0)
        e.memset(epscol[:], 1e-5)
        e.memset(onescol[:], 1.0)
        e.memset(KVsb[:], 0.0)
        e.memset(KVpad32[:], 0.0)
        for p_ in range(2):
            for h in range(16):
                e.memset(vaugHp[p_][:, 65 * h + 64: 65 * h + 65], 1.0)
                e.memset(vaugLp[p_][:, 65 * h + 64: 65 * h + 65], 0.0)
        return e.engine_nop()
    vrun(init_const, writes=("ones1", "ones1b", "onescol", "KVsb", "KVpad32",
                             "vaugH0", "vaugL0", "vaugH1", "vaugL1", "epscol"))

    dma_in(rowstage[0:1, 0:1], dws[0:1, :], bufw=("rowstage",))

    def ws_mm(e):
        return e.matmul(pb[0][:, 0:1], ones1[:], rowstage[0:1, 0:1], start=True, stop=True)
    trun(ws_mm, reads=("rowstage", "ones1"), writes=("pb0",))

    def ws_ev(e):
        return e.tensor_copy(wscol[:], pb[0][:, 0:1])
    vrun(ws_ev, reads=("pb0",), writes=("wscol",))

    for bq4 in range(FF // 1024):
        dma_in(rowstage[0:1, 0:1024], db1[0:1, 1024 * bq4: 1024 * (bq4 + 1)],
               bufw=("rowstage",))

        def b1mm(e):
            last = None
            for f in range(8):
                last = e.matmul(pb[0][:, f: f + 1],
                                rowstage[0:1, 128 * f: 128 * (f + 1)],
                                onescol[:], start=True, stop=True)
            return last
        trun(b1mm, reads=("rowstage", "onescol"), writes=("pb0",))

        def b1ev(e, bq4=bq4):
            return e.tensor_copy(b1cols[:, 8 * bq4: 8 * (bq4 + 1)], pb[0][:, 0:8])
        vrun(b1ev, reads=("pb0",), writes=("b1cols",))


    load_weight_pairs(dWk, W1h, W1l, "W1h", "W1l", 0, 0)
    load_weight_pairs(dWv, W2h, W2l, "W2h", "W2l", 0, 0)
    stage_bias_splits(dbk, rowh[0:1, :], rowl[0:1, :], "rowh", "rowl")
    stage_bias_splits(dbv, rowh2[0:1, :], rowl2[0:1, :], "rowh2", "rowl2")

    def a_front(i):
        xb = i % 2
        p = i % 2
        xname = f"xt{xb}"
        xnb, sH, sL = xnbp[p], sHp[p], sLp[p]
        dma_in(xt[xb][:], xr[128 * i: 128 * (i + 1), :], bufw=(xname,))
        ln_stats(xt[xb][:], (xname,), m1s[:, i: i + 1], rs1[:, i: i + 1],
                 f"m1s{i}", f"rs1{i}")

        def xnfn(e, i=i, xb=xb, xnb=xnb, sH=sH, sL=sL):
            e.tensor_scalar(xnb[:], xt[xb][:], m1s[:, i: i + 1], rs1[:, i: i + 1],
                            AL.subtract, AL.mult)
            e.tensor_copy(sH[:], xnb[:])
            return e.tensor_sub(sL[:], xnb[:], sH[:])
        vrun(xnfn, reads=(xname, f"m1s{i}", f"rs1{i}"),
             writes=(f"xnb{p}", f"sH{p}", f"sL{p}"))

    def a_transp(i):
        p = i % 2
        transp(fH, "fH", sHp[p][:], (f"sH{p}",), KC)
        transp(fL, "fL", sLp[p][:], (f"sL{p}",), KC)

    a_front(0)
    a_transp(0)
    proj_tm(fH[:], fL[:], ("fH", "fL"), W1h, W1l, "W1h", "W1l",
            (rowh, rowl, "rowh", "rowl"), banks=(0, 1))
    for i in range(NT_FULL):
        p = i % 2
        sH, sL = sHp[p], sLp[p]
        FA, FB = FAp[p], FBp[p]
        kH, kL = kHp[p], kLp[p]
        vaugH, vaugL = vaugHp[p], vaugLp[p]
        nFA, nFB = f"FA{p}", f"FB{p}"
        proj_tm(fH[:], fL[:], ("fH", "fL"), W2h, W2l, "W2h", "W2l",
                (rowh2, rowl2, "rowh2", "rowl2"), banks=(2, 3))
        if i + 1 < NT_FULL:
            a_front(i + 1)
        for hlf in range(2):
            def kel(e, hlf=hlf, FA=FA, FB=FB):
                bap = pb[hlf][:, :]
                e.tensor_scalar(FA[:, 512 * hlf: 512 * (hlf + 1)], bap, 0.0, None, AL.min)
                return e.tensor_scalar(FB[:, 512 * hlf: 512 * (hlf + 1)], bap, 0.0,
                                       None, AL.max)
            vrun(kel, reads=(f"pb{hlf}",), writes=(nFA, nFB))

        def kexp(e, FA=FA):
            return e.activation(FE[:], FA[:], AF.Exp)
        srun(kexp, reads=(nFA,), writes=("FE",))

        def ksum(e, FB=FB, kH=kH, kL=kL):
            e.tensor_add(FC[:], FE[:], FB[:])
            e.tensor_copy(kH[:], FC[:])
            return e.tensor_sub(kL[:], FC[:], kH[:])
        vrun(ksum, reads=("FE", nFB), writes=("FC", f"kH{p}", f"kL{p}"))
        for hlf in range(2):
            def vev(e, hlf=hlf):
                return e.tensor_copy(FD[:, 512 * hlf: 512 * (hlf + 1)],
                                     pb[2 + hlf][:, :])
            vrun(vev, reads=(f"pb{2 + hlf}",), writes=("FD",))

        def vsplit(e, vaugH=vaugH, vaugL=vaugL):
            vaugH_d = vaugH[:].rearrange("p (h m) -> p h m", m=65)[:, :, 0:64]
            vaugL_d = vaugL[:].rearrange("p (h m) -> p h m", m=65)[:, :, 0:64]
            fd3 = FD[:].rearrange("p (h m) -> p h m", m=64)
            e.tensor_copy(vaugH_d, fd3)
            return e.tensor_tensor(vaugL_d, fd3, vaugH_d, AL.subtract)
        vrun(vsplit, reads=("FD",), writes=(f"vaugH{p}", f"vaugL{p}"))
        if i + 1 < NT_FULL:
            a_transp(i + 1)
            proj_tm(fH[:], fL[:], ("fH", "fL"), W1h, W1l, "W1h", "W1l",
                    (rowh, rowl, "rowh", "rowl"), banks=(0, 1))

        def kvmm(e, kH=kH, kL=kL, vaugH=vaugH, vaugL=vaugL):
            last = None
            for h in range(16):
                pair, odd = h // 2, h % 2
                bank = 4 if pair < 4 else 5
                col = 65 * (pair % 4)
                out_ap = pb[bank][64 * odd: 64 * odd + 64, col: col + 65]
                for pi, (a, bsrc) in enumerate(((kH, vaugH), (kH, vaugL), (kL, vaugH))):
                    last = e.matmul(out_ap, a[:, 64 * h: 64 * h + 64],
                                    bsrc[:, 65 * h: 65 * h + 65],
                                    start=(pi == 0), stop=(pi == 2),
                                    tile_position=(0, 64 * odd))
            return last
        trun(kvmm, reads=(f"kH{p}", f"kL{p}", f"vaugH{p}", f"vaugL{p}"),
             writes=("pb4", "pb5"))

        def kvacc(e):
            e.tensor_tensor(KVsb[:, 0:260], pb[4][:, 0:260], KVsb[:, 0:260], AL.add)
            return e.tensor_tensor(KVsb[:, 260:520], pb[5][:, 0:260],
                                   KVsb[:, 260:520], AL.add)
        vrun(kvacc, reads=("pb4", "pb5", "KVsb"), writes=("KVsb",))

    kv3 = KVsb[:].rearrange("p (q m) -> p q m", m=65)
    pad3 = KVpad32[:].rearrange("p (h m) -> p h m", m=65)
    for q_ in range(8):
        dma_out(pad3[0:64, 2 * q_: 2 * q_ + 1, :], kv3[0:64, q_: q_ + 1, :],
                bufr=("KVsb",), bufw=("KVpad32",))
        dma_out(pad3[64:128, 2 * q_ + 1: 2 * q_ + 2, :], kv3[64:128, q_: q_ + 1, :],
                bufr=("KVsb",), bufw=("KVpad32",))

    def kvsplit(e):
        e.tensor_copy(KVpadh[:], KVpad32[:])
        return e.tensor_sub(KVpadl[:], KVpad32[:], KVpadh[:])
    vrun(kvsplit, reads=("KVpad32",), writes=("KVpadh", "KVpadl"))

    load_weight_pairs(dWq, W1h, W1l, "W1h", "W1l", 0, 0)
    load_weight_pairs(dWo, W2h, W2l, "W2h", "W2l", 0, 0)
    stage_bias_splits(dbq, rowh[0:1, :], rowl[0:1, :], "rowh", "rowl")
    stage_bias_splits(dbo, rowh2[0:1, :], rowl2[0:1, :], "rowh2", "rowl2")
    build_rep(dg1, 0)
    build_rep(dbe1, 1)

    def b_front(i):
        xb = i % 2
        p = i % 2
        xname = f"xt{xb}"
        xnb, sH, sL = xnbp[p], sHp[p], sLp[p]
        dma_in(xt[xb][:], xr[128 * i: 128 * (i + 1), :], bufw=(xname,))

        def xnfn(e, i=i, xb=xb, xnb=xnb, sH=sH, sL=sL):
            e.tensor_scalar(xnb[:], xt[xb][:], m1s[:, i: i + 1], rs1[:, i: i + 1],
                            AL.subtract, AL.mult)
            e.tensor_copy(sH[:], xnb[:])
            return e.tensor_sub(sL[:], xnb[:], sH[:])
        vrun(xnfn, reads=(xname, f"m1s{i}", f"rs1{i}"),
             writes=(f"xnb{p}", f"sH{p}", f"sL{p}"))

    def b_tQ(i):
        p = i % 2
        transp(fH, "fH", sHp[p][:], (f"sH{p}",), KC)
        transp(fL, "fL", sLp[p][:], (f"sL{p}",), KC)

    def b_qproj(i):
        p = i % 2
        qb = (0, 1) if p == 0 else (4, 5)
        proj_tm(fH[:], fL[:], ("fH", "fL"), W1h, W1l, "W1h", "W1l",
                (rowh, rowl, "rowh", "rowl"), banks=qb)

    def b_elu(i):
        p = i % 2
        FA, FB = FAp[p], FBp[p]
        nFA, nFB = f"FA{p}", f"FB{p}"
        qb = (0, 1) if p == 0 else (4, 5)
        sH, sL = sHp[p], sLp[p]
        for hlf in range(2):
            def qel(e, hlf=hlf, FA=FA, FB=FB, qb=qb):
                bap = pb[qb[hlf]][:, :]
                e.tensor_scalar(FA[:, 512 * hlf: 512 * (hlf + 1)], bap, 0.0, None, AL.min)
                return e.tensor_scalar(FB[:, 512 * hlf: 512 * (hlf + 1)], bap, 0.0,
                                       None, AL.max)
            vrun(qel, reads=(f"pb{qb[hlf]}",), writes=(nFA, nFB))

        def qexp(e, FA=FA):
            return e.activation(FE[:], FA[:], AF.Exp)
        srun(qexp, reads=(nFA,), writes=("FE",))

        def qsum(e, FB=FB, sH=sH, sL=sL):
            e.tensor_add(FC[:], FE[:], FB[:])
            e.tensor_copy(sH[:], FC[:])
            return e.tensor_sub(sL[:], FC[:], sH[:])
        vrun(qsum, reads=("FE", nFB), writes=("FC", f"sH{p}", f"sL{p}"))

    def b_qk_group(i, grp):
        p = i % 2
        bank = 2 + (grp % 2)
        bname = f"pb{bank}"

        def qkmm(e, grp=grp, bank=bank):
            last = None
            for hh in range(4):
                h = 4 * grp + hh
                c = h // 2
                out_ap = pb[bank][:, 65 * hh: 65 * hh + 65]
                for pi, (a, bsrc) in enumerate(((fH, KVpadh), (fH, KVpadl),
                                                (fL, KVpadh))):
                    last = e.matmul(out_ap, a[:, 128 * c: 128 * (c + 1)],
                                    bsrc[:, 65 * h: 65 * h + 65],
                                    start=(pi == 0), stop=(pi == 2))
            return last
        trun(qkmm, reads=("fH", "fL", "KVpadh", "KVpadl"), writes=(bname,))

        def zrec(e, grp=grp, bank=bank):
            den = pb[bank][:, 64: 64 + 4 * 65].rearrange(
                "p (h m) -> p h m", m=65)[:, :, 0:1]
            zsl = zc[:, 4 * grp: 4 * grp + 4]
            i1 = e.tensor_scalar(zsl, den, 1e-6, None, AL.add)
            vdrain(e, i1)
            r = e.reciprocal(zsl, zsl)
            vdrain(e, r)
            return e.engine_nop()
        vrun(zrec, reads=(bname,), writes=("zc",))

        def osc(e, grp=grp, bank=bank):
            last = None
            for hh in range(4):
                h = 4 * grp + hh
                last = e.tensor_scalar(FD[:, 64 * h: 64 * h + 64],
                                       pb[bank][:, 65 * hh: 65 * hh + 64],
                                       zc[:, h: h + 1], None, AL.mult)
            return last
        vrun(osc, reads=(bname, "zc"), writes=("FD",))

    def b_osplit(i):
        p = i % 2
        sH, sL = sHp[p], sLp[p]

        def osplit(e, sH=sH, sL=sL):
            e.tensor_copy(sH[:], FD[:])
            return e.tensor_sub(sL[:], FD[:], sH[:])
        vrun(osplit, reads=("FD",), writes=(f"sH{p}", f"sL{p}"))

    def b_attn(i):
        p = i % 2
        qb = (0, 1) if p == 0 else (4, 5)
        proj_tm(fH[:], fL[:], ("fH", "fL"), W2h, W2l, "W2h", "W2l",
                (rowh2, rowl2, "rowh2", "rowl2"), banks=qb)

    def b_tail(i):
        p = i % 2
        qb = (0, 1) if p == 0 else (4, 5)
        FA, FB = FAp[p], FBp[p]
        nFA, nFB = f"FA{p}", f"FB{p}"
        xnb, sH, sL = xnbp[p], sHp[p], sLp[p]
        for hlf in range(2):
            def r1fn(e, hlf=hlf, FA=FA, xnb=xnb, qb=qb):
                return e.tensor_add(FA[:, 512 * hlf: 512 * (hlf + 1)],
                                    pb[qb[hlf]][:, :],
                                    xnb[:, 512 * hlf: 512 * (hlf + 1)])
            vrun(r1fn, reads=(f"pb{qb[hlf]}", f"xnb{p}"), writes=(nFA,))
        ln_stats(FA[:], (nFA,), mcp[p][:], rcp[p][:], f"mc{p}", f"rc{p}")

        def x1fn(e, FA=FA, FB=FB, sH=sH, sL=sL, p=p):
            e.tensor_scalar(FB[:], FA[:], mcp[p][:], rcp[p][:], AL.subtract, AL.mult)
            e.tensor_mul(FB[:], FB[:], rep[0][:])
            e.tensor_add(FB[:], FB[:], rep[1][:])
            e.tensor_copy(sH[:], FB[:])
            return e.tensor_sub(sL[:], FB[:], sH[:])
        vrun(x1fn, reads=(nFA, f"mc{p}", f"rc{p}", "rep0", "rep1"),
             writes=(nFB, f"sH{p}", f"sL{p}"))
        dma_out(dx1t_h[128 * i: 128 * (i + 1), :], sH[:], bufr=(f"sH{p}",),
                bufw=(f"d_x1th{i}",))
        dma_out(dx1t_l[128 * i: 128 * (i + 1), :], sL[:], bufr=(f"sL{p}",),
                bufw=(f"d_x1tl{i}",))

    def b_x1f(i):
        p = i % 2
        transp(fH, "fH", sHp[p][:], (f"sH{p}",), KC)
        transp(fL, "fL", sLp[p][:], (f"sL{p}",), KC)
        dma_out(dx1f_h[i, :, :], fH[:], bufr=("fH",), bufw=(f"d_x1fh{i}",))
        dma_out(dx1f_l[i, :, :], fL[:], bufr=("fL",), bufw=(f"d_x1fl{i}",))

    b_front(0)
    b_front(1)
    for j in range(NT_OWN // 2):
        a_, b_ = 2 * j, 2 * j + 1
        b_tQ(a_)
        b_qproj(a_)
        b_tQ(b_)
        b_elu(a_)
        b_qproj(b_)
        b_tQ(a_)
        b_elu(b_)
        for grp in range(4):
            b_qk_group(a_, grp)
        b_osplit(a_)
        b_tQ(b_)
        for grp in range(4):
            b_qk_group(b_, grp)
        b_tQ(a_)
        b_attn(a_)
        b_osplit(b_)
        b_tQ(b_)
        b_tail(a_)
        b_attn(b_)
        b_x1f(a_)
        b_tail(b_)
        if a_ + 2 < NT_OWN:
            b_front(a_ + 2)
        b_x1f(b_)
        if b_ + 2 < NT_OWN:
            b_front(b_ + 2)

    build_rep(db2, 2)
    build_rep(dg2, 0)
    build_rep(dbe2, 1)

    def d1_tile_thunks(i):
        th = []
        xb = i % 2
        p = i % 2
        xname = f"xt{xb}"
        FA, FB = FAp[p], FBp[p]
        nFA, nFB = f"FA{p}", f"FB{p}"
        sH, kH, kL = sHp[p], kHp[p], kLp[p]
        def t_loads():
            dma_in(xt[xb][:], xr[128 * i: 128 * (i + 1), :], bufw=(xname,))
            dma_in(kH[:], dx1t_h[128 * i: 128 * (i + 1), :], bufw=(f"kH{p}",),
                   bufr=(f"d_x1th{i}",))
            dma_in(kL[:], dx1t_l[128 * i: 128 * (i + 1), :], bufw=(f"kL{p}",),
                   bufr=(f"d_x1tl{i}",))
            dma_in(FE[:], dyacc[128 * i: 128 * (i + 1), :], bufw=("FE",),
                   bufr=(f"d_yacc{i}",))
        th.append(t_loads)

        def t_r2(FA=FA, kH=kH, kL=kL):
            def r2fn(e, FA=FA, kH=kH, kL=kL):
                e.tensor_add(FA[:], kH[:], kL[:])
                return e.tensor_add(FA[:], FA[:], FE[:])
            vrun(r2fn, reads=(f"kH{p}", f"kL{p}", "FE"), writes=(nFA,))
        th.append(t_r2)
        th.append(lambda: ln_stats(FA[:], (nFA,), mcp[p][:], rcp[p][:],
                                   f"mc{p}", f"rc{p}"))

        def t_xmid():
            def xmid(e, xb=xb, FA=FA, FB=FB, p=p):
                e.tensor_scalar(FB[:], FA[:], mcp[p][:], rcp[p][:], AL.subtract,
                                AL.mult)
                e.tensor_mul(FB[:], FB[:], rep[0][:])
                e.tensor_add(FB[:], FB[:], rep[1][:])
                return e.tensor_add(FB[:], FB[:], xt[xb][:])
            vrun(xmid, reads=(nFA, f"mc{p}", f"rc{p}", "rep0", "rep1", xname),
                 writes=(nFB,))
            dma_out(dxmid[128 * i: 128 * (i + 1), :], FB[:], bufr=(nFB,),
                    bufw=(f"d_xmid{'e' if i % 2 == 0 else 'o'}{i}",))
        th.append(t_xmid)
        th.append(lambda: ln_stats(FB[:], (nFB,), mcp[p][:], rcp[p][:],
                                   f"mc{p}", f"rc{p}"))

        def t_qgen():
            def qgen(e, FB=FB, p=p, i=i):
                e.tensor_scalar(FC[:], FB[:], mcp[p][:], rcp[p][:], AL.subtract,
                                AL.mult)
                i1 = e.reduce_max(amc[:], FC[:], axis=AX.X,
                                  apply_absolute_value=True)
                vdrain(e, i1)
                i2 = e.tensor_scalar(amc[:], amc[:], 1e-5, None, AL.max)
                vdrain(e, i2)
                return e.scalar_tensor_tensor(sfall[:, i: i + 1], amc[:],
                                              1.0 / 127.0, wscol[:],
                                              AL.mult, AL.mult)
            vrun(qgen, reads=(nFB, f"mc{p}", f"rc{p}", "wscol"),
                 writes=("FC", "amc", "sfall"))

            def asfn(e):
                i1 = e.reciprocal(asc[:], amc[:])
                vdrain(e, i1)
                r = e.tensor_scalar(asc[:], asc[:], 127.0, None, AL.mult)
                vdrain(e, r)
                return e.engine_nop()
            vrun(asfn, reads=("amc",), writes=("asc",))
        th.append(t_qgen)

        def t_lad0():
            def lad0(e):
                e.tensor_scalar(FD[:], FC[:], asc[:], 128.5, AL.mult, AL.add)
                return e.tensor_copy(FE[:], FD[:])
            vrun(lad0, reads=("FC", "asc"), writes=("FD", "FE"))
        th.append(t_lad0)

        def t_lad1():
            def ladder_a(e):
                for k in range(7, 3, -1):
                    e.tensor_scalar(FC[:], FE[:], float(2 ** k), float(-(2 ** k)),
                                    AL.is_ge, AL.mult)
                    e.tensor_add(FE[:], FE[:], FC[:])
                return e.engine_nop()
            vrun(ladder_a, reads=("FE",), writes=("FE", "FC"))
        th.append(t_lad1)

        def t_lad2(sH=sH):
            def ladder_b(e, sH=sH):
                for k in range(3, -1, -1):
                    e.tensor_scalar(FC[:], FE[:], float(2 ** k), float(-(2 ** k)),
                                    AL.is_ge, AL.mult)
                    e.tensor_add(FE[:], FE[:], FC[:])
                e.tensor_sub(FD[:], FD[:], FE[:])
                return e.tensor_scalar(sH[:], FD[:], -128.0, None, AL.add)
            vrun(ladder_b, reads=("FD", "FE"), writes=("FD", "FE", "FC", f"sH{p}"))
            dma_out(dqt[128 * i: 128 * (i + 1), :], sH[:], bufr=(f"sH{p}",),
                    bufw=(f"d_qt{'e' if i % 2 == 0 else 'o'}{i}",))
        th.append(t_lad2)
        return th

    NE = NQ
    d1pend = []
    for qr in range(NQ):
        load_weight_pairs(dW1, W1h, W1l, "W1h", "W1l", 0, 1024 * qr)
        load_weight_pairs(dW2, W2h, W2l, "W2h", "W2l", 1024 * qr, 0)
        for sc_ in range(OWN // 256):
            bh, bl = "x1sb0", "x1sb1"
            sbh, sbl = x1sb[0], x1sb[1]
            dma_in(sbh[:, 0:1024], dx1f_h[2 * sc_, :, :], bufw=(bh,),
                   bufr=(f"d_x1fh{2 * sc_}",))
            dma_in(sbh[:, 1024:2048], dx1f_h[2 * sc_ + 1, :, :], bufw=(bh,),
                   bufr=(f"d_x1fh{2 * sc_ + 1}",))
            dma_in(sbl[:, 0:1024], dx1f_l[2 * sc_, :, :], bufw=(bl,),
                   bufr=(f"d_x1fl{2 * sc_}",))
            dma_in(sbl[:, 1024:2048], dx1f_l[2 * sc_ + 1, :, :], bufw=(bl,),
                   bufr=(f"d_x1fl{2 * sc_ + 1}",))
            rh3 = sbh[:].rearrange("p (s c) -> p s c", c=1024)
            rl3 = sbl[:].rearrange("p (s c) -> p s c", c=1024)
            hbank = [(pb[4], "pb4"), (pb[5], "pb5"), (tp[0], "tp0"), (tp[1], "tp1")]

            def emit_hmm(ffc):
                hb_ap, hbn = hbank[ffc % 4]
                for c in range(KC):
                    def hmm(e, ffc=ffc, hb_ap=hb_ap, rh3=rh3, rl3=rl3, c=c):
                        last = None
                        base = 1024 * c + 128 * ffc
                        for pi, (wt, xs3) in enumerate(((W1h, rh3), (W1l, rh3),
                                                        (W1h, rl3))):
                            last = e.matmul(
                                hb_ap[:, 0:256],
                                wt[:, base: base + 128],
                                xs3[:, :, 128 * c: 128 * (c + 1)],
                                start=(c == 0 and pi == 0),
                                stop=(c == KC - 1 and pi == 2))
                        return last
                    trun(hmm, reads=(bh, bl, f"W1h:c{c}", f"W1l:c{c}"),
                         writes=(hbn,))

            for f0 in range(4):
                emit_hmm(f0)
            for ffc in range(8):
                if d1pend:
                    d1pend.pop(0)()
                hb_ap, hbn = hbank[ffc % 4]
                fp = ffc % 2
                hH, hL = hHp[fp], hLp[fp]
                gtmp = gt[fp]
                gname = f"gt{fp}"

                def gel(e, ffc=ffc, hb_ap=hb_ap, qr=qr, gtmp=gtmp):
                    return e.activation(gtmp[:, 0:256], hb_ap[:, 0:256], AF.Gelu,
                                        bias=b1cols[:, 8 * qr + ffc: 8 * qr + ffc + 1])
                srun(gel, reads=(hbn, "b1cols"), writes=(gname,))

                def hsp(e, hH=hH, hL=hL, gtmp=gtmp):
                    e.tensor_copy(hH[:], gtmp[:, 0:256])
                    return e.tensor_sub(hL[:], gtmp[:, 0:256], hH[:])
                vrun(hsp, reads=(gname,), writes=(f"hH{fp}", f"hL{fp}"))

                if ffc + 4 < 8:
                    emit_hmm(ffc + 4)

                def ymm(e, ffc=ffc, hH=hH, hL=hL):
                    last = None
                    base0 = 1024 * ffc
                    for sub in range(2):
                        for hlf in range(2):
                            yb = 2 * sub + hlf
                            for a, wt in ((hH, W2h), (hH, W2l), (hL, W2h)):
                                last = e.matmul(
                                    pb[yb][:, :],
                                    a[:, 128 * sub: 128 * (sub + 1)],
                                    wt[:, base0 + 512 * hlf: base0 + 512 * hlf + 512],
                                    start=(ffc == 0 and a is hH and wt is W2h),
                                    stop=(ffc == 7 and a is hL))
                    return last
                trun(ymm, reads=(f"hH{fp}", f"hL{fp}", f"W2h:c{ffc}",
                                 f"W2l:c{ffc}"),
                     writes=("pb0", "pb1", "pb2", "pb3"))
            for sub in range(2):
                for hlf in range(2):
                    yb = 2 * sub + hlf

                    def yev(e, hlf=hlf, yb=yb, qr=qr):
                        dst = ysb[:, 512 * hlf: 512 * hlf + 512]
                        if qr == 0:
                            return e.tensor_add(dst, pb[yb][:, :],
                                                rep[2][:, 512 * hlf: 512 * hlf + 512])
                        return e.tensor_copy(dst, pb[yb][:, :])
                    vrun(yev, reads=((f"pb{yb}", "rep2") if qr == 0
                                     else (f"pb{yb}",)), writes=("ysb",))
                dma_out(dyacc[256 * sc_ + 128 * sub: 256 * sc_ + 128 * (sub + 1), :],
                        ysb[:],
                        bufr=("ysb",), bufw=(f"d_yacc{2 * sc_ + sub}",),
                        accum=(qr > 0))
            if qr == NE - 1:
                d1pend.extend(d1_tile_thunks(2 * sc_))

    build_rep(dbf, 2)
    for c in range(KC):
        dma_in(tmpi[:], dPT[128 * c: 128 * (c + 1), :], bufw=("tmpi",))
        for s4 in range(4):
            def ufn(e, c=c, s4=s4):
                e.tensor_scalar(tmpi2[:], tmpi[:], 2 * s4, 3,
                                AL.logical_shift_right, AL.bitwise_and)
                return e.tensor_scalar(
                    W1h[:, 1024 * c + 256 * s4: 1024 * c + 256 * (s4 + 1)],
                    tmpi2[:], -1.0, None, AL.add)
            vrun(ufn, reads=("tmpi",), writes=(f"W1h:c{c}", "tmpi2"))
        if d1pend:
            d1pend.pop(0)()
    def d2_tile(i):
        p = i % 2
        FA, FB = FAp[p], FBp[p]
        nFA, nFB = f"FA{p}", f"FB{p}"
        sL, fHx = sLp[p], (fH if p == 0 else fL)
        tb = (0, 1) if p == 0 else (4, 5)
        dma_in(sL[:], dqt[128 * i: 128 * (i + 1), :], bufw=(f"sL{p}",),
               bufr=(f"d_qt{'e' if i % 2 == 0 else 'o'}{i}",))
        dma_in(FB[:], dxmid[128 * i: 128 * (i + 1), :], bufw=(nFB,),
               bufr=(f"d_xmid{'e' if i % 2 == 0 else 'o'}{i}",))
        transp(fHx, ("fH" if p == 0 else "fL"), sL[:], (f"sL{p}",), KC)

        def tmm(e, tb=tb, fHx=fHx):
            last = None
            for hlf in range(2):
                first = True
                for c in range(KC):
                    last = e.matmul(pb[tb[hlf]][:, :], fHx[:, 128 * c: 128 * (c + 1)],
                                    W1h[:, 1024 * c + 512 * hlf: 1024 * c + 512 * hlf + 512],
                                    start=first, stop=(c == KC - 1))
                    first = False
            return last
        trun(tmm, reads=(("fH" if p == 0 else "fL"),)
             + tuple(f"W1h:c{c}" for c in range(KC)),
             writes=(f"pb{tb[0]}", f"pb{tb[1]}"))

        def dfin(e, FA=FA, FB=FB, tb=tb, i=i):
            last = None
            for hlf in range(2):
                sl = slice(512 * hlf, 512 * (hlf + 1))
                e.scalar_tensor_tensor(FA[:, sl], pb[tb[hlf]][:, :],
                                       sfall[:, i: i + 1],
                                       rep[2][:, sl], AL.mult, AL.add)
                last = e.tensor_add(FA[:, sl], FA[:, sl], FB[:, sl])
            return last
        vrun(dfin, reads=(f"pb{tb[0]}", f"pb{tb[1]}", "sfall", "rep2", nFB),
             writes=(nFA,))
        dma_out(dout[128 * i: 128 * (i + 1), :], FA[:], bufr=(nFA,), bufw=("dout",))

    while d1pend:
        d1pend.pop(0)()
    pend = []
    for j in range(NT_OWN // 2):
        d2_tile(2 * j)
        pend.extend(d1_tile_thunks(2 * j + 1))
        for _ in range(4):
            if pend:
                pend.pop(0)()
    while pend:
        pend.pop(0)()
    for j in range(NT_OWN // 2):
        d2_tile(2 * j + 1)

    final_out = P.dsem.get("D:dout", 0)

    semstack = ExitStack()
    sems = {}
    for name in ("tensor", "vector", "scalar", "gpsimd"):
        sems[name] = semstack.enter_context(nc.semaphore("sem_" + name))
    for key in P.dsem:
        sems[key] = semstack.enter_context(
            nc.semaphore("sem_" + key.replace(":", "_")))
    sems["vself"] = semstack.enter_context(nc.semaphore("sem_vself"))
    VSELF["sem"] = sems["vself"]

    with nc.Block() as block:
        def emit(eng_obj, name):
            achieved = {}
            for idx, (engine, fn, waits, kind) in enumerate(P.steps):
                if engine != name:
                    continue
                for sname, val in waits:
                    v = val * 16 if sname.startswith("D:") else val
                    if achieved.get(sname, -1) >= v:
                        continue
                    eng_obj.wait_ge(sems[sname], v)
                    achieved[sname] = v
                ret = fn(eng_obj)
                if kind == "eng":
                    ret.then_inc(sems[name], 1)
                else:
                    key = P.step_dkey[idx]
                    for ins in ret:
                        ins.then_inc(sems[key], 16)

        @block.sync
        def _(sync):
            emit(sync, "sync")

        @block.vector
        def _(vector):
            emit(vector, "vector")

        @block.scalar
        def _(scalar):
            emit(scalar, "scalar")

        @block.tensor
        def _(tensor):
            emit(tensor, "tensor")

        @block.gpsimd
        def _(gpsimd):
            emit(gpsimd, "gpsimd")
            gpsimd.wait_ge(sems["D:dout"], final_out * 16)

    semstack.close()
    es.close()
    return nc


def kernel(L0_attn_src, L0_attn_Wq, L0_attn_bq, L0_attn_Wk, L0_attn_bk,
           L0_attn_Wv, L0_attn_bv, L0_attn_Wo, L0_attn_bo, L0_attn_W1,
           L0_attn_b1, L0_attn_W2, L0_attn_b2, L0_attn_g1, L0_attn_be1,
           L0_attn_g2, L0_attn_be2, L0_ffn_packed_weight, L0_ffn_weight_scale,
           L0_ffn_bias):
    x = np.ascontiguousarray(np.asarray(L0_attn_src, dtype=np.float32))
    PT = np.ascontiguousarray(
        np.asarray(L0_ffn_packed_weight, dtype=np.int32).reshape(256, 1024).T)
    row = lambda a, n: np.ascontiguousarray(np.asarray(a, np.float32).reshape(1, n))
    common = {
        "Wq": np.ascontiguousarray(np.asarray(L0_attn_Wq, np.float32)),
        "Wk": np.ascontiguousarray(np.asarray(L0_attn_Wk, np.float32)),
        "Wv": np.ascontiguousarray(np.asarray(L0_attn_Wv, np.float32)),
        "Wo": np.ascontiguousarray(np.asarray(L0_attn_Wo, np.float32)),
        "W1": np.ascontiguousarray(np.asarray(L0_attn_W1, np.float32)),
        "W2": np.ascontiguousarray(np.asarray(L0_attn_W2, np.float32)),
        "bq": row(L0_attn_bq, D), "bk": row(L0_attn_bk, D),
        "bv": row(L0_attn_bv, D), "bo": row(L0_attn_bo, D),
        "b1": row(L0_attn_b1, FF), "b2": row(L0_attn_b2, D),
        "g1": row(L0_attn_g1, D), "be1": row(L0_attn_be1, D),
        "g2": row(L0_attn_g2, D), "be2": row(L0_attn_be2, D),
        "PT": PT,
        "ws": np.asarray(L0_ffn_weight_scale, np.float32).reshape(1, 1),
        "bf": row(L0_ffn_bias, D),
        "id16": np.eye(128).astype(ml_dtypes.bfloat16),
    }
    in_maps = []
    for core in range(8):
        b_, j = core // 2, core % 2
        own = x[b_, j * OWN: (j + 1) * OWN]
        other = x[b_, (1 - j) * OWN: (2 - j) * OWN]
        m = dict(common)
        m["xr"] = np.ascontiguousarray(np.concatenate([own, other], axis=0))
        in_maps.append(m)

    nc = build_nc()
    res = run_bass_kernel_spmd(nc, in_maps, core_ids=list(range(8)))
    out = np.empty((B, S, D), dtype=np.float32)
    for core in range(8):
        b_, j = core // 2, core % 2
        out[b_, j * OWN: (j + 1) * OWN] = res.results[core]["out"]
    return out
